# version 04
# baseline (speedup 1.0000x reference)
import numpy as np
import ml_dtypes

import concourse.bass as bass
import concourse.mybir as mybir
from concourse.bass_utils import run_bass_kernel_spmd

F32 = mybir.dt.float32
BF16 = mybir.dt.bfloat16
AF = mybir.ActivationFunctionType
ALU = mybir.AluOpType
AX = mybir.AxisListType
AP = bass.AP

B, N, D, H, HD = 4, 4096, 1024, 16, 64
MY = N // 2
DC = D // 128
HID = 4096
HC = HID // 128
FSB = 512
ASB = 512
LN_EPS = 1e-5
RMS_EPS = 1e-6
BASE = 10000.0
NCORES = 8

ENGS = ("tensor", "vector", "scalar", "gpsimd", "sync")


class Plan:
    def __init__(self, nc):
        self.nc = nc
        self.steps = {e: [] for e in ENGS}
        self.semnames = []
        self.cnt = {}

    def sem(self, name):
        self.semnames.append(name)
        self.cnt[name] = 0
        return name

    ENG_SEM = {"vector": "sVE", "scalar": "sSC", "gpsimd": "sGP"}

    def emit(self, eng, fn, waits=(), inc=None, k=1, is_barrier=False):
        if not is_barrier and eng in self.ENG_SEM and inc not in ("sLD", "sLG", "sST"):
            sem = self.ENG_SEM[eng]
            if self.cnt[sem] > 0:
                waits = list(waits) + [(sem, self.cnt[sem])]
            inc = sem
            k = 1
        tick = None
        if inc is not None:
            self.cnt[inc] += k
            tick = self.cnt[inc]
        self.steps[eng].append((fn, tuple(waits), inc, k))
        return tick

    def dma(self, eng, out, in_, sem, waits=(), transpose=False, slow=False):
        if transpose:
            fn = lambda e, o=out, i=in_: e.dma_start_transpose(out=o, in_=i)
        else:
            fn = lambda e, o=out, i=in_: e.dma_start(
                out=o, in_=i, allow_slow_non_contiguous=slow)
        return self.emit(eng, fn, waits, sem, 16, is_barrier=True)

    def mm(self, out, lhsT, rhs, start, stop, waits=(), tile_position=None):
        fn = lambda e, o=out, l=lhsT, r=rhs: e.matmul(
            o, l, r, start=start, stop=stop, skip_group_check=True,
            tile_position=tile_position)
        return self.emit("tensor", fn, waits, "sMM")

    def barrier(self):
        for e in ENGS:
            self.emit(e, lambda en: en.nop(), (), "s_" + e, is_barrier=True)
        snap = dict(self.cnt)
        for e in ENGS:
            waits = [(s, snap[s]) for s in self.semnames if snap[s] > 0]
            self.emit(e, lambda en: None, waits, is_barrier=True)

    def realize(self):
        nc = self.nc
        sems = {}
        for n in self.semnames:
            sems[n] = nc.semaphore(n).__enter__()
        with nc.Block() as blk:
            def body(e, eng_name):
                waited = {}
                for fn, waits, inc, k in self.steps[eng_name]:
                    for s, t in waits:
                        if t is None or t <= 0 or waited.get(s, 0) >= t:
                            continue
                        e.wait_ge(sems[s], t)
                        waited[s] = t
                    inst = fn(e)
                    if inc is not None:
                        inst.then_inc(sems[inc], k)

            blk.tensor(lambda e: body(e, "tensor"))
            blk.vector(lambda e: body(e, "vector"))
            blk.scalar(lambda e: body(e, "scalar"))
            blk.gpsimd(lambda e: body(e, "gpsimd"))
            blk.sync(lambda e: body(e, "sync"))


def build_program(upto="full", debug=()):
    debug = set(debug)
    nc = bass.Bass("TRN2", target_bir_lowering=False)
    P = Plan(nc)
    sLD = P.sem("sLD"); sLG = P.sem("sLG"); sST = P.sem("sST")
    sMM = P.sem("sMM"); sVE = P.sem("sVE"); sSC = P.sem("sSC"); sGP = P.sem("sGP")
    for e in ENGS:
        P.sem("s_" + e)

    d_x = nc.dram_tensor("xT", [D, N], F32, kind="ExternalInput")
    d_wq = nc.dram_tensor("Wq", [D, D], F32, kind="ExternalInput")
    d_wk = nc.dram_tensor("Wk", [D, D], F32, kind="ExternalInput")
    d_wv = nc.dram_tensor("Wv", [D, D], F32, kind="ExternalInput")
    d_wo = nc.dram_tensor("Wo", [D, D], F32, kind="ExternalInput")
    d_wg1 = nc.dram_tensor("Wg1", [D, H], F32, kind="ExternalInput")
    d_wg2 = nc.dram_tensor("Wg2", [H, D], F32, kind="ExternalInput")
    d_w1 = nc.dram_tensor("w1", [D, HID], F32, kind="ExternalInput")
    d_w2 = nc.dram_tensor("w2", [HID, D], F32, kind="ExternalInput")
    d_par = nc.dram_tensor("par", [128, 129], F32, kind="ExternalInput")
    d_b2r = nc.dram_tensor("b2row", [1, D], F32, kind="ExternalInput")
    d_trigf = nc.dram_tensor("trigf", [128, MY], F32, kind="ExternalInput")
    d_trigtm = nc.dram_tensor("trigtm", [MY, 128], BF16, kind="ExternalInput")
    d_trigtp = nc.dram_tensor("trigtp", [MY, 128], BF16, kind="ExternalInput")
    d_mask = nc.dram_tensor("mask", [128, 128], BF16, kind="ExternalInput")
    d_ident = nc.dram_tensor("ident", [128, 128], F32, kind="ExternalInput")
    d_dup0 = nc.dram_tensor("dup0", [128, 128], BF16, kind="ExternalInput")
    d_dup1 = nc.dram_tensor("dup1", [128, 128], BF16, kind="ExternalInput")

    d_out = nc.dram_tensor("out", [D, MY], F32, kind="ExternalOutput")
    d_dbg = {}

    def dbg_out(name, shape, dt):
        d_dbg[name] = nc.dram_tensor(name, shape, dt, kind="ExternalOutput")
        return d_dbg[name]

    if "qz" in debug:
        dbg_out("dbg_sc", [128, 8], F32)
        dbg_out("dbg_q1s", [128, HID], BF16)
        dbg_out("dbg_q2s", [128, D], BF16)
    if "lnin" in debug:
        dbg_out("dbg_xn", [128, DC * FSB], BF16)
        dbg_out("dbg_h", [128, FSB], BF16)
        dbg_out("dbg_h2", [128, FSB], BF16)
    if "x1" in debug:
        dbg_out("dbg_x1", [D, N], F32)
    if "xn2" in debug:
        dbg_out("dbg_xn2", [D, N], BF16)
    if "ev" in debug:
        dbg_out("dbg_e", [MY // 128, 128, D], BF16)
        dbg_out("dbg_v", [MY // 128, 128, D], BF16)
        dbg_out("dbg_T", [128, 1024], F32)
        dbg_out("dbg_carry", [128, DC], F32)
    if "x2" in debug:
        dbg_out("dbg_x2", [D, MY], F32)
    if "p4s" in debug:
        dbg_out("dbg_oT", [128, DC * ASB], BF16)
        dbg_out("dbg_qnT", [128, H * ASB], BF16)
        dbg_out("dbg_ekT", [128, H * ASB], BF16)
        dbg_out("dbg_R", [128, DC * ASB], F32)
        dbg_out("dbg_ektk", [128, (ASB // 128) * 2048], BF16)
        dbg_out("dbg_gate", [128, DC * ASB], BF16)
        dbg_out("dbg_trigf", [128, MY], F32)
        dbg_out("dbg_dup", [128, 512], F32)
        dbg_out("dbg_eT", [128, DC * ASB], BF16)


    s_q1 = nc.dram_tensor("q1d", [D, HID], BF16)
    s_q2 = nc.dram_tensor("q2d", [HID, D], BF16)
    s_x1 = nc.dram_tensor("x1d", [D, N], F32)
    s_xn2 = nc.dram_tensor("xn2d", [D, N], BF16)
    s_xn2t = nc.dram_tensor("xn2td", [N // 128, 128, D], BF16)
    s_etok = nc.dram_tensor("etokd", [MY // 128, 128, D], BF16)
    s_vtok = nc.dram_tensor("vtokd", [MY // 128, 128, D], BF16)
    s_x2 = nc.dram_tensor("x2d", [D, MY], F32)
    s_crow = nc.dram_tensor("crowd", [1, D], F32)

    alloc = nc.alloc_sbuf_tensor
    sb_par = alloc("sb_par", [128, 129], F32)
    sb_b2r = alloc("sb_b2r", [1, D], F32)
    sb_mask = alloc("sb_mask", [128, 512], BF16)
    sb_id = alloc("sb_id", [128, 128], F32)
    sb_dup = alloc("sb_dup", [128, 256], BF16)
    sb_ones_b = alloc("sb_ones_b", [128, 1], BF16)
    sb_ones_f = alloc("sb_ones_f", [128, 1], F32)
    sb_onesrow = alloc("sb_onesrow", [1, 512], F32)
    sb_zero_b = alloc("sb_zero_b", [128, 512], BF16)
    sb_rows = alloc("sb_rows", [1, 2048], F32)
    sb_rows2 = alloc("sb_rows2", [1, 1536], F32)
    sb_sc = alloc("sb_sc", [128, 8], F32)
    sb_carry = alloc("sb_carry", [128, DC], F32)
    sb_Tinit = alloc("sb_Tinit", [128, 1024], F32)
    sb_Tsnap = alloc("sb_Tsnap", [128, 2048], BF16)
    sb_trigf = alloc("sb_trigf", [128, MY], F32)
    gamma_ap = sb_par[:, 128:129]

    P.dma("sync", sb_par[:, :], d_par[:, :], sLD)
    P.dma("sync", sb_b2r[:, :], d_b2r[:, :], sLD)
    for r in range(4):
        P.dma("sync", sb_mask[:, r * 128:(r + 1) * 128], d_mask[:, :], sLD)
    P.dma("sync", sb_id[:, :], d_ident[:, :], sLD)
    P.dma("sync", sb_dup[:, 0:128], d_dup0[:, :], sLD)
    P.dma("sync", sb_dup[:, 128:256], d_dup1[:, :], sLD)
    P.dma("sync", sb_trigf[:, :], d_trigf[:, :], sLD)
    for fn in (lambda e: e.memset(sb_ones_b[:, :], 1.0),
               lambda e: e.memset(sb_ones_f[:, :], 1.0),
               lambda e: e.memset(sb_onesrow[:, :], 1.0),
               lambda e: e.memset(sb_zero_b[:, :], 0.0)):
        P.emit("vector", fn, (), sVE)
    P.barrier()

    def quantize(widx, d_w, s_q, nslab, ncols, scol):
        from contextlib import ExitStack
        with ExitStack() as es:
            slab = es.enter_context(nc.sbuf_tensor(f"qz_slab{widx}", [128, 2 * ncols], F32))
            work = es.enter_context(nc.sbuf_tensor(f"qz_work{widx}", [128, 2 * ncols], F32))
            acc = es.enter_context(nc.sbuf_tensor(f"qz_acc{widx}", [128, nslab + 1], F32))
            qt = es.enter_context(nc.sbuf_tensor(f"qz_q{widx}", [128, 2 * ncols], BF16))
            row = es.enter_context(nc.sbuf_tensor(f"qz_row{widx}", [1, 512], F32))
            accrow = es.enter_context(nc.sbuf_tensor(f"qz_ar{widx}", [1, 128], F32))
            ps = es.enter_context(nc.psum_tensor(f"qz_ps{widx}", [128, 512], F32))

            ve_t = [None] * nslab
            for s in range(nslab):
                half = (s % 2) * ncols
                w = [] if s < 2 else [(sVE, ve_t[s - 2])]
                tl = P.dma("sync", slab[:, half:half + ncols],
                           d_w[s * 128:(s + 1) * 128, :], sLD, w)
                ve_t[s] = P.emit("vector", lambda e, s=s, half=half: e.tensor_reduce(
                    out=acc[:, s:s + 1], in_=slab[:, half:half + ncols],
                    axis=AX.X, op=ALU.add, apply_absolute_value=True),
                    [(sLD, tl)], sVE)
            tv = P.emit("vector", lambda e: e.tensor_reduce(
                out=acc[:, nslab:nslab + 1], in_=acc[:, 0:nslab], axis=AX.X,
                op=ALU.add), (), sVE)
            t1 = P.dma("gpsimd", AP(s_crow, 0, [[1, 128], [1, 1]]),
                       acc[:, nslab:nslab + 1], sST, [(sVE, tv)])
            t2 = P.dma("sync", accrow[0:1, :], s_crow[0:1, 0:128], sLD, [(sST, t1)])
            tv = P.emit("vector", lambda e: e.tensor_reduce(
                out=row[0:1, 0:1], in_=accrow[0:1, :], axis=AX.X, op=ALU.add),
                [(sLD, t2)], sVE)
            tv = P.emit("vector", lambda e: e.tensor_scalar(
                out=row[0:1, 0:1], in0=row[0:1, 0:1],
                scalar1=1.0 / (128.0 * nslab * ncols), scalar2=1e-5,
                op0=ALU.mult, op1=ALU.max), (), sVE)
            tv = P.emit("vector", lambda e: e.tensor_scalar(
                out=row[0:1, 1:2], in0=row[0:1, 0:1], scalar1=0.5, scalar2=None,
                op0=ALU.mult), (), sVE)
            tm = P.mm(ps[:, 0:512], sb_onesrow[0:1, 0:128], row[0:1, 0:512], True, True,
                      [(sVE, tv)])
            tv = P.emit("vector", lambda e: e.tensor_copy(
                out=sb_sc[:, scol:scol + 2], in_=ps[:, 0:2]), [(sMM, tm)], sVE)

            st_t = [None] * nslab
            qt_t = [None] * nslab
            for s in range(nslab):
                half = (s % 2) * ncols
                w = [(sVE, tv)] if s < 2 else [(sST, st_t[s - 2])]
                tl = P.dma("sync", slab[:, half:half + ncols],
                           d_w[s * 128:(s + 1) * 128, :], sLD, w)
                wsg = [(sLD, tl)]
                if s >= 2:
                    wsg.append((sVE, qt_t[s - 2]))
                tsg = P.emit("scalar", lambda e, half=half: e.sign(
                    out=work[:, half:half + ncols], in_=slab[:, half:half + ncols]),
                    wsg, sSC)
                tab = P.emit("scalar", lambda e, half=half: e.activation(
                    out=slab[:, half:half + ncols], in_=slab[:, half:half + ncols],
                    func=AF.Abs), (), sSC)
                P.emit("vector", lambda e, half=half: e.tensor_single_scalar(
                    out=slab[:, half:half + ncols], in_=slab[:, half:half + ncols],
                    scalar=sb_sc[:, scol + 1:scol + 2], op=ALU.is_gt),
                    [(sSC, tab)], sVE)
                wq = [] if s < 2 else [(sST, st_t[s - 2])]
                tq = P.emit("vector", lambda e, half=half: e.tensor_tensor(
                    out=qt[:, half:half + ncols], in0=slab[:, half:half + ncols],
                    in1=work[:, half:half + ncols], op=ALU.mult), wq, sVE)
                qt_t[s] = tq
                st_t[s] = P.dma("gpsimd", s_q[s * 128:(s + 1) * 128, :],
                                qt[:, half:half + ncols], sST, [(sVE, tq)])

    quantize(0, d_w1, s_q1, DC, HID, 0)
    P.barrier()
    quantize(1, d_w2, s_q2, HC, D, 2)
    P.emit("vector", lambda e: e.tensor_scalar(
        out=sb_sc[:, 4:5], in0=sb_sc[:, 2:3], scalar1=0.5, scalar2=None,
        op0=ALU.mult), (), sVE)
    P.emit("vector", lambda e: e.reciprocal(out=sb_sc[:, 5:6], in_=sb_sc[:, 2:3]),
           (), sVE)
    P.barrier()

    if "dbg_sc" in d_dbg:
        P.dma("gpsimd", d_dbg["dbg_sc"][:, :], sb_sc[:, :], sST)
        P.dma("gpsimd", d_dbg["dbg_q1s"][:, :], s_q1[0:128, :], sST)
        P.dma("gpsimd", d_dbg["dbg_q2s"][:, :], s_q2[0:128, :], sST)
        P.barrier()
    if upto == "p0":
        P.realize()
        return nc, d_dbg

    def ffn_phase(tag, tokens, in_dram, epilogue):
        from contextlib import ExitStack
        with ExitStack() as es:
            xblk = es.enter_context(nc.sbuf_tensor(f"fx{tag}", [128, DC * FSB], F32))
            xbf = es.enter_context(nc.sbuf_tensor(f"fxb{tag}", [128, DC * FSB], BF16))
            xn = es.enter_context(nc.sbuf_tensor(f"fxn{tag}", [128, DC * FSB], BF16))
            hbuf = es.enter_context(nc.sbuf_tensor(f"fh{tag}", [128, HC * FSB], BF16))
            sq = es.enter_context(nc.sbuf_tensor(f"fsq{tag}", [128, 2 * FSB], BF16))
            f32s = es.enter_context(nc.sbuf_tensor(f"f32s{tag}", [128, FSB], F32))
            cfA = es.enter_context(nc.sbuf_tensor(f"fcA{tag}", [128, FSB], F32))
            cfB = es.enter_context(nc.sbuf_tensor(f"fcB{tag}", [128, FSB], F32))
            wt = es.enter_context(nc.sbuf_tensor(f"fwt{tag}", [128, 2 * HID], BF16))
            x1blk = es.enter_context(nc.sbuf_tensor(f"fx1{tag}", [128, DC * FSB], F32))
            ps = es.enter_context(nc.psum_tensor(f"fps{tag}", [128, 2048], F32))
            pr = [es.enter_context(nc.psum_tensor(f"fpr{tag}{i}", [1, 512], F32))
                  for i in range(4)]
            bufs = dict(xblk=xblk, xbf=xbf, xn=xn, hbuf=hbuf, sq=sq, cfA=cfA,
                        cfB=cfB, wt=wt, x1blk=x1blk, ps=ps, pr=pr, f32s=f32s)
            for sb in range(tokens // FSB):
                _ffn_sb(tag, sb, sb * FSB, in_dram, epilogue, bufs)
                P.barrier()

    def ln_stats(src_f32, src_w, dst_bf, dst_sq, pr, width):
        gp_t, sc_t = [], []
        for ch in range(DC):
            gp_t.append(P.emit("gpsimd", lambda e, ch=ch: e.tensor_copy(
                out=dst_bf[:, ch * width:(ch + 1) * width],
                in_=src_f32[:, ch * width:(ch + 1) * width]),
                [(src_w[0], src_w[1][ch])] if src_w else (), sGP))
        for ch in range(DC):
            sc_t.append(P.emit("scalar", lambda e, ch=ch: e.activation(
                out=dst_sq[:, ch * width:(ch + 1) * width],
                in_=dst_bf[:, ch * width:(ch + 1) * width], func=AF.Square),
                [(sGP, gp_t[ch])], sSC))
        tm = None
        for ch in range(DC):
            for tq in range(width // 512):
                c = ch * width + tq * 512
                tm = P.mm(pr[tq][0:1, :], sb_ones_b[:, 0:1], dst_bf[:, c:c + 512],
                          ch == 0, ch == DC - 1, [(sGP, gp_t[ch])])
                tm = P.mm(pr[2 + tq][0:1, :], sb_ones_b[:, 0:1], dst_sq[:, c:c + 512],
                          ch == 0, ch == DC - 1, [(sSC, sc_t[ch])])
        return tm

    def ln_rows(pr, Rm, Rv, Ra, Rb, width, denom, fused, tm_wait):
        for tq in range(width // 512):
            P.emit("vector", lambda e, tq=tq: e.tensor_scalar(
                out=Rm[0:1, tq * 512:(tq + 1) * 512], in0=pr[tq][0:1, :],
                scalar1=1.0 / denom, scalar2=None, op0=ALU.mult), [(sMM, tm_wait)], sVE)
            P.emit("vector", lambda e, tq=tq: e.tensor_scalar(
                out=Rv[0:1, tq * 512:(tq + 1) * 512], in0=pr[2 + tq][0:1, :],
                scalar1=1.0 / denom, scalar2=None, op0=ALU.mult), [(sMM, tm_wait)], sVE)
        P.emit("vector", lambda e: e.tensor_tensor(out=Ra, in0=Rm, in1=Rm, op=ALU.mult), (), sVE)
        P.emit("vector", lambda e: e.tensor_tensor(out=Rv, in0=Rv, in1=Ra, op=ALU.subtract), (), sVE)
        P.emit("vector", lambda e: e.tensor_scalar(out=Ra, in0=Rv, scalar1=LN_EPS,
                                                   scalar2=None, op0=ALU.add), (), sVE)
        if fused:
            P.emit("vector", lambda e: e.reciprocal(out=Rb, in_=Ra), (), sVE)
            P.emit("vector", lambda e: e.tensor_tensor(out=Rb, in0=Rv, in1=Rb, op=ALU.mult), (), sVE)
            P.emit("vector", lambda e: e.tensor_scalar(out=Rb, in0=Rb, scalar1=LN_EPS,
                                                       scalar2=None, op0=ALU.add), (), sVE)
            P.emit("vector", lambda e: e.tensor_tensor(out=Ra, in0=Ra, in1=Rb, op=ALU.mult), (), sVE)
        tsc = P.emit("scalar", lambda e: e.activation(out=Ra, in_=Ra, func=AF.Sqrt),
                     [(sVE, P.cnt[sVE])], sSC)
        P.emit("vector", lambda e: e.reciprocal(out=Ra, in_=Ra), [(sSC, tsc)], sVE)
        P.emit("vector", lambda e: e.tensor_tensor(out=Rb, in0=Rm, in1=Ra, op=ALU.mult), (), sVE)
        tv = P.emit("vector", lambda e: e.tensor_scalar(out=Rb, in0=Rb, scalar1=-1.0,
                                                        scalar2=None, op0=ALU.mult), (), sVE)
        return tv

    def bcast_fields(Ra, Rb, cfA, cfB, width, ps, wv):
        last = None
        for tq in range(width // 512):
            t1 = P.mm(ps[:, tq * 512:(tq + 1) * 512], sb_onesrow[0:1, 0:128],
                      Ra[0:1, tq * 512:(tq + 1) * 512], True, True, [(sVE, wv)])
            t2 = P.mm(ps[:, 1024 + tq * 512:1024 + (tq + 1) * 512], sb_onesrow[0:1, 0:128],
                      Rb[0:1, tq * 512:(tq + 1) * 512], True, True, [(sVE, wv)])
            P.emit("scalar", lambda e, tq=tq: e.copy(
                out=cfA[:, tq * 512:(tq + 1) * 512], in_=ps[:, tq * 512:(tq + 1) * 512]),
                [(sMM, t1)], sSC)
            last = P.emit("scalar", lambda e, tq=tq: e.copy(
                out=cfB[:, tq * 512:(tq + 1) * 512],
                in_=ps[:, 1024 + tq * 512:1024 + (tq + 1) * 512]), [(sMM, t2)], sSC)
        return last

    def _ffn_sb(tag, sb, col0, in_dram, epilogue, bufs):
        xblk, xbf, xn, hbuf, sq = bufs["xblk"], bufs["xbf"], bufs["xn"], bufs["hbuf"], bufs["sq"]
        cfA, cfB, wt, x1blk, ps, pr = (bufs["cfA"], bufs["cfB"], bufs["wt"],
                                       bufs["x1blk"], bufs["ps"], bufs["pr"])
        f32s = bufs["f32s"]
        ld_t = []
        for ch in range(DC):
            ld_t.append(P.dma("sync", xblk[:, ch * FSB:(ch + 1) * FSB],
                              in_dram[ch * 128:(ch + 1) * 128, col0:col0 + FSB], sLD))
        tm = ln_stats(xblk, (sLD, ld_t), xbf, xn, pr, FSB)
        R = lambda i: sb_rows[0:1, i * FSB:(i + 1) * FSB]
        tv = ln_rows(pr, R(0), R(1), R(2), R(3), FSB, D, True, tm)
        tf = bcast_fields(R(2), R(3), cfA, cfB, FSB, ps, tv)
        sc_xn = []
        for ch in range(DC):
            wv1 = [(sSC, tf)]
            if ch >= 1:
                wv1.append((sSC, sc_xn[ch - 1]))
            P.emit("vector", lambda e, ch=ch: e.tensor_tensor(
                out=f32s[:, :], in0=xblk[:, ch * FSB:(ch + 1) * FSB],
                in1=cfA[:, :], op=ALU.mult), wv1, sVE)
            t = P.emit("vector", lambda e, ch=ch: e.tensor_tensor(
                out=f32s[:, :], in0=f32s[:, :],
                in1=cfB[:, :], op=ALU.add), (), sVE)
            t = P.emit("scalar", lambda e, ch=ch: e.activation(
                out=xn[:, ch * FSB:(ch + 1) * FSB], in_=f32s[:, :],
                func=AF.Identity, bias=sb_par[:, 8 + ch:9 + ch],
                scale=sb_par[:, ch:ch + 1]), [(sVE, t)], sSC)
            sc_xn.append(t)

        if tag == 1 and sb == 0 and "dbg_xn" in d_dbg:
            P.dma("gpsimd", d_dbg["dbg_xn"][:, :], xn[:, :], sST, [(sSC, sc_xn[DC - 1])])
        evac_t = {}
        sq_mm = {}
        wslab_t = {}
        for hb in range(HC):
            if hb % 2 == 0:
                hbp = hb // 2
                w = [] if hbp < 2 else [(sSC, evac_t[hb - 3])]
                for hbi in range(2):
                    wslab_t[hbp] = P.dma(
                        "sync",
                        AP(wt, (hbp % 2) * 2048 + hbi * 1024,
                           [[2 * HID, 128], [128, DC], [1, 128]]),
                        AP(s_q1, (hb + hbi) * 128, [[HID, 128], [128 * HID, DC], [1, 128]]),
                        sLD, w)
            par = hb % 2
            hbp = hb // 2
            pcol = par * 1024
            wv0 = [(sLD, wslab_t[hbp]), (sSC, sc_xn[0])]
            if hb >= 2:
                wv0.append((sSC, evac_t[hb - 2]))
            for dc in range(DC):
                lhs = wt[:, (hbp % 2) * 2048 + (hb % 2) * 1024 + dc * 128:
                         (hbp % 2) * 2048 + (hb % 2) * 1024 + (dc + 1) * 128]
                wv = wv0 if dc == 0 else [(sSC, sc_xn[dc])]
                for tq in range(FSB // 512):
                    P.mm(ps[:, pcol + tq * 512:pcol + (tq + 1) * 512], lhs,
                         xn[:, dc * FSB + tq * 512:dc * FSB + (tq + 1) * 512],
                         dc == 0, dc == DC - 1, wv)
            tmm = P.cnt[sMM]
            tev = P.emit("scalar", lambda e, hb=hb, pcol=pcol: e.activation(
                out=hbuf[:, hb * FSB:(hb + 1) * FSB], in_=ps[:, pcol:pcol + FSB],
                func=AF.Gelu, bias=sb_par[:, 16 + hb:17 + hb], scale=sb_sc[:, 0:1]),
                [(sMM, tmm)], sSC)
            evac_t[hb] = tev
            if tag == 1 and sb == 0 and hb == 0 and "dbg_h" in d_dbg:
                P.dma("gpsimd", d_dbg["dbg_h"][:, :], hbuf[:, 0:FSB], sST, [(sSC, tev)])
        tsq_t = []
        for hb in range(HC):
            tsq_t.append(P.emit("vector", lambda e, hb=hb: e.tensor_tensor(
                out=sq[:, (hb % 2) * FSB:(hb % 2 + 1) * FSB],
                in0=hbuf[:, hb * FSB:(hb + 1) * FSB],
                in1=hbuf[:, hb * FSB:(hb + 1) * FSB], op=ALU.mult),
                [(sSC, evac_t[hb])] if hb < 2 else
                [(sSC, evac_t[hb]), (sMM, sq_mm[hb - 2])], sVE))
            P.mm(pr[0][0:1, :], sb_ones_b[:, 0:1],
                 hbuf[:, hb * FSB:hb * FSB + 512],
                 hb == 0, hb == HC - 1, [(sSC, evac_t[hb])])
            sq_mm[hb] = P.mm(pr[2][0:1, :], sb_ones_b[:, 0:1],
                             sq[:, (hb % 2) * FSB:(hb % 2) * FSB + 512],
                             hb == 0, hb == HC - 1, [(sVE, tsq_t[hb])])
        R2m = sb_rows2[0:1, 0:FSB]
        R2v = sb_rows2[0:1, FSB:2 * FSB]
        Ra = sb_rows[0:1, 0:FSB]
        Rb = sb_rows[0:1, FSB:2 * FSB]
        tm2 = P.cnt[sMM]
        for tq in range(FSB // 512):
            P.emit("vector", lambda e, tq=tq: e.tensor_scalar(
                out=R2m[0:1, tq * 512:(tq + 1) * 512], in0=pr[tq][0:1, :],
                scalar1=1.0 / HID, scalar2=None, op0=ALU.mult), [(sMM, tm2)], sVE)
            P.emit("vector", lambda e, tq=tq: e.tensor_scalar(
                out=R2v[0:1, tq * 512:(tq + 1) * 512], in0=pr[2 + tq][0:1, :],
                scalar1=1.0 / HID, scalar2=None, op0=ALU.mult), [(sMM, tm2)], sVE)
        P.emit("vector", lambda e: e.tensor_tensor(out=Ra, in0=R2m, in1=R2m, op=ALU.mult), (), sVE)
        P.emit("vector", lambda e: e.tensor_tensor(out=R2v, in0=R2v, in1=Ra, op=ALU.subtract), (), sVE)
        P.emit("vector", lambda e: e.tensor_scalar(out=Ra, in0=R2v, scalar1=LN_EPS,
                                                   scalar2=None, op0=ALU.add), (), sVE)
        tsc = P.emit("scalar", lambda e: e.activation(out=Ra, in_=Ra, func=AF.Sqrt),
                     [(sVE, P.cnt[sVE])], sSC)
        P.emit("vector", lambda e: e.reciprocal(out=Ra, in_=Ra), [(sSC, tsc)], sVE)
        tv = P.emit("vector", lambda e: e.tensor_scalar(out=Rb, in0=R2m, scalar1=-1.0,
                                                        scalar2=None, op0=ALU.mult), (), sVE)
        tf2 = bcast_fields(Ra, Rb, cfA, cfB, FSB, ps, tv)
        h2_t = {}
        for hc in range(HC):
            wv1 = [(sSC, tf2)]
            if hc >= 1:
                wv1.append((sSC, h2_t[hc - 1]))
            P.emit("vector", lambda e, hc=hc: e.tensor_tensor(
                out=f32s[:, :], in0=hbuf[:, hc * FSB:(hc + 1) * FSB],
                in1=cfB[:, :], op=ALU.add), wv1, sVE)
            t = P.emit("vector", lambda e, hc=hc: e.tensor_tensor(
                out=f32s[:, :], in0=f32s[:, :],
                in1=cfA[:, :], op=ALU.mult), (), sVE)
            h2_t[hc] = P.emit("scalar", lambda e, hc=hc: e.activation(
                out=hbuf[:, hc * FSB:(hc + 1) * FSB], in_=f32s[:, :],
                func=AF.Identity, bias=sb_par[:, 80 + hc:81 + hc],
                scale=sb_par[:, 48 + hc:49 + hc]), [(sVE, t)], sSC)
        if tag == 1 and sb == 0 and "dbg_h2" in d_dbg:
            P.dma("gpsimd", d_dbg["dbg_h2"][:, :], hbuf[:, 0:FSB], sST, [(sSC, h2_t[0])])
        tiv = P.emit("vector", lambda e: e.tensor_scalar(
            out=sb_rows2[0:1, 0:512], in0=sb_onesrow[0:1, 0:512],
            scalar1=sb_sc[0:1, 5:6], scalar2=None, op0=ALU.mult), (), sVE)
        x1_t = [None] * DC
        wslab2_t = {}
        for db in range(DC):
            par = db % 2
            pcol = par * 1024
            w = [(sSC, h2_t[0])] if db < 2 else [(sVE, x1_t[db - 2])]
            wslab2_t[db] = P.dma("sync",
                                 AP(wt, (db % 2) * HID, [[2 * HID, 128], [128, HC], [1, 128]]),
                                 AP(s_q2, db * 128, [[D, 128], [128 * D, HC], [1, 128]]),
                                 sLD, w)
            for hc in range(HC):
                lhs = wt[:, par * HID + hc * 128:par * HID + (hc + 1) * 128]
                wv = [(sLD, wslab2_t[db]), (sSC, h2_t[hc])]
                if hc == 0 and db >= 2:
                    wv.append((sVE, x1_t[db - 2]))
                for tq in range(FSB // 512):
                    P.mm(ps[:, pcol + tq * 512:pcol + (tq + 1) * 512], lhs,
                         hbuf[:, hc * FSB + tq * 512:hc * FSB + (tq + 1) * 512],
                         hc == 0, False, wv)
            for tq in range(FSB // 512):
                P.mm(ps[:, pcol + tq * 512:pcol + (tq + 1) * 512],
                     sb_b2r[0:1, db * 128:(db + 1) * 128],
                     sb_rows2[0:1, 0:512], False, True, [(sVE, tiv)])
            tmm = P.cnt[sMM]
            x1_t[db] = P.emit("vector", lambda e, db=db, pcol=pcol: e.scalar_tensor_tensor(
                out=x1blk[:, db * FSB:(db + 1) * FSB], in0=ps[:, pcol:pcol + FSB],
                scalar=sb_sc[:, 4:5], in1=xblk[:, db * FSB:(db + 1) * FSB],
                op0=ALU.mult, op1=ALU.add), [(sMM, tmm)], sVE)
        epilogue(sb, col0, bufs, x1_t)

    def p1_epilogue(sb, col0, bufs, x1_t):
        xbf, xn, x1blk, ps, pr, cfA, cfB = (bufs["xbf"], bufs["xn"], bufs["x1blk"],
                                            bufs["ps"], bufs["pr"], bufs["cfA"], bufs["cfB"])
        for ch in range(DC):
            P.dma("gpsimd", s_x1[ch * 128:(ch + 1) * 128, col0:col0 + FSB],
                  x1blk[:, ch * FSB:(ch + 1) * FSB], sST, [(sVE, x1_t[ch])])
            if "dbg_x1" in d_dbg:
                P.dma("gpsimd", d_dbg["dbg_x1"][ch * 128:(ch + 1) * 128, col0:col0 + FSB],
                      x1blk[:, ch * FSB:(ch + 1) * FSB], sST, [(sVE, x1_t[ch])])
        tm = ln_stats(x1blk, (sVE, x1_t), xbf, xn, pr, FSB)
        R = lambda i: sb_rows[0:1, i * FSB:(i + 1) * FSB]
        tv = ln_rows(pr, R(0), R(1), R(2), R(3), FSB, D, False, tm)
        tf = bcast_fields(R(2), R(3), cfA, cfB, FSB, ps, tv)
        f32s = bufs["f32s"]
        ve_t = []
        for ch in range(DC):
            P.emit("vector", lambda e, ch=ch: e.tensor_tensor(
                out=f32s[:, :], in0=x1blk[:, ch * FSB:(ch + 1) * FSB],
                in1=cfA[:, :], op=ALU.mult), [(sSC, tf)], sVE)
            ve_t.append(P.emit("vector", lambda e, ch=ch: e.tensor_tensor(
                out=xn[:, ch * FSB:(ch + 1) * FSB], in0=f32s[:, :],
                in1=cfB[:, :], op=ALU.add), (), sVE))
        for ch in range(DC):
            P.dma("gpsimd", s_xn2[ch * 128:(ch + 1) * 128, col0:col0 + FSB],
                  xn[:, ch * FSB:(ch + 1) * FSB], sST, [(sVE, ve_t[ch])])
            if "dbg_xn2" in d_dbg:
                P.dma("gpsimd", d_dbg["dbg_xn2"][ch * 128:(ch + 1) * 128, col0:col0 + FSB],
                      xn[:, ch * FSB:(ch + 1) * FSB], sST, [(sVE, ve_t[ch])])
        for tt in range(FSB // 128):
            tile = (col0 + tt * 128) // 128
            src = AP(xn, tt * 128, [[DC * FSB, 128], [FSB, DC], [1, 128]])
            dst = AP(s_xn2t, tile * 128 * D, [[D, 128], [128, DC], [1, 128]])
            P.dma("gpsimd", dst, src, sST, [(sVE, ve_t[DC - 1])])

    ffn_phase(1, N, d_x, p1_epilogue)
    if upto == "p1":
        P.realize()
        return nc, d_dbg

    def p2_phase():
        from contextlib import ExitStack
        with ExitStack() as es:
            wkv = es.enter_context(nc.sbuf_tensor("p2_wkv", [128, DC * 2048], BF16))
            xt = es.enter_context(nc.sbuf_tensor("p2_xt", [128, 2 * D], BF16))
            et = es.enter_context(nc.sbuf_tensor("p2_et", [128, 2 * 1024], BF16))
            vt = es.enter_context(nc.sbuf_tensor("p2_vt", [128, 2 * 1024], BF16))
            ek = es.enter_context(nc.sbuf_tensor("p2_ek", [128, 2048], BF16))
            trg = es.enter_context(nc.sbuf_tensor("p2_trg", [128, 16 * 128], BF16))
            crow = es.enter_context(nc.sbuf_tensor("p2_crow", [1, 1024], F32))
            c128 = es.enter_context(nc.sbuf_tensor("p2_c128", [128, DC], F32))
            ps_kv = es.enter_context(nc.psum_tensor("p2_pskv", [128, 2048], F32))
            ps_T = es.enter_context(nc.psum_tensor("p2_psT", [128, 1024], F32))
            ps_c = es.enter_context(nc.psum_tensor("p2_psc", [1, 1024], F32))

            wld = []
            for dc in range(DC):
                wld.append(P.dma("gpsimd", wkv[:, dc * 2048:dc * 2048 + 1024],
                                 d_wk[dc * 128:(dc + 1) * 128, :], sLG))
                wld.append(P.dma("gpsimd", wkv[:, dc * 2048 + 1024:dc * 2048 + 2048],
                                 d_wv[dc * 128:(dc + 1) * 128, :], sLG))
            ttr = P.dma("sync", AP(trg, 0, [[16 * 128, 128], [128, 16], [1, 128]]),
                        AP(d_trigtp, 0, [[128, 128], [128 * 128, 16], [1, 128]]), sLD)
            P.mm(ps_T[:, 0:512], sb_zero_b[:, 0:128], sb_zero_b[:, 0:512], True, True)
            P.mm(ps_T[:, 512:1024], sb_zero_b[:, 0:128], sb_zero_b[:, 0:512], True, True)

            prev = {}
            for t in range(N // 128):
                par = t % 2
                w = [] if t < 2 else [(sMM, prev[t - 2]["lastmm"])]
                tlx = P.dma("sync", xt[:, par * D:(par + 1) * D],
                            AP(s_xn2t, t * 128 * D, [[D, 128], [1, D]]), sLD, w)
                wv = [(sLD, tlx), (sLG, wld[-1])]
                if t >= 1:
                    wv.append((sSC, prev[t - 1]["esc"]))
                    wv.append((sVE, prev[t - 1]["vve"]))
                tmm = None
                for dc in range(DC):
                    for q in range(4):
                        tmm = P.mm(ps_kv[:, q * 512:(q + 1) * 512],
                                   xt[:, par * D + dc * 128:par * D + (dc + 1) * 128],
                                   wkv[:, dc * 2048 + q * 512:dc * 2048 + (q + 1) * 512],
                                   dc == 0, dc == DC - 1, wv if dc == 0 else ())
                esc = P.emit("scalar", lambda e, par=par: e.activation(
                    out=et[:, par * 1024:(par + 1) * 1024], in_=ps_kv[:, 0:1024],
                    func=AF.Exp), [(sMM, tmm)], sSC)
                vve = P.emit("vector", lambda e, par=par: e.tensor_copy(
                    out=vt[:, par * 1024:(par + 1) * 1024], in_=ps_kv[:, 1024:2048]),
                    [(sMM, tmm)], sVE)
                prev[t] = dict(lastmm=tmm, esc=esc, vve=vve)

                if t < MY // 128:
                    for q in range(2):
                        P.mm(ps_c[0:1, q * 512:(q + 1) * 512], sb_ones_b[:, 0:1],
                             et[:, par * 1024 + q * 512:par * 1024 + (q + 1) * 512],
                             t == 0, t == MY // 128 - 1, [(sSC, esc)])
                    ek_c = AP(ek, 0, [[2048, 128], [128, H], [1, 64]])
                    ek_s = AP(ek, 64, [[2048, 128], [128, H], [1, 64]])
                    e_in = AP(et, par * 1024, [[2 * 1024, 128], [64, H], [1, 64]])
                    cs = AP(trg, t * 128, [[16 * 128, 128], [0, H], [1, 64]])
                    sn = AP(trg, t * 128 + 64, [[16 * 128, 128], [0, H], [1, 64]])
                    P.emit("vector", lambda e, a=ek_c, b=e_in, c=cs: e.tensor_tensor(
                        out=a, in0=b, in1=c, op=ALU.mult),
                        [(sSC, esc), (sLD, ttr)], sVE)
                    tve = P.emit("vector", lambda e, a=ek_s, b=e_in, c=sn: e.tensor_tensor(
                        out=a, in0=b, in1=c, op=ALU.mult), (), sVE)
                    for h in range(H):
                        tmm2 = P.mm(ps_T[:, h * 64:(h + 1) * 64],
                                    ek[:, h * 128:(h + 1) * 128],
                                    vt[:, par * 1024 + h * 64:par * 1024 + (h + 1) * 64],
                                    False, False,
                                    [(sVE, tve), (sVE, vve)] if h == 0 else ())
                    prev[t]["lastmm"] = tmm2
                else:
                    ti = t - MY // 128
                    P.dma("gpsimd", AP(s_etok, ti * 128 * D, [[D, 128], [1, D]]),
                          et[:, par * 1024:(par + 1) * 1024], sST, [(sSC, esc)])
                    P.dma("gpsimd", AP(s_vtok, ti * 128 * D, [[D, 128], [1, D]]),
                          vt[:, par * 1024:(par + 1) * 1024], sST, [(sVE, vve)])
                    if "dbg_e" in d_dbg:
                        P.dma("gpsimd", AP(d_dbg["dbg_e"], ti * 128 * D, [[D, 128], [1, D]]),
                              et[:, par * 1024:(par + 1) * 1024], sST, [(sSC, esc)])
                        P.dma("gpsimd", AP(d_dbg["dbg_v"], ti * 128 * D, [[D, 128], [1, D]]),
                              vt[:, par * 1024:(par + 1) * 1024], sST, [(sVE, vve)])

            tmm_all = P.cnt[sMM]
            tTi = P.emit("vector", lambda e: e.tensor_scalar(
                out=sb_Tinit[:, :], in0=ps_T[:, :], scalar1=gamma_ap, scalar2=None,
                op0=ALU.mult), [(sMM, tmm_all)], sVE)
            tcr = P.emit("vector", lambda e: e.tensor_copy(out=crow[0:1, :], in_=ps_c[0:1, :]),
                         [(sMM, tmm_all)], sVE)
            tst = P.dma("gpsimd", s_crow[0:1, :], crow[0:1, :], sST, [(sVE, tcr)])
            tld = P.dma("sync", c128[:, :], AP(s_crow, 0, [[1, 128], [128, DC]]), sLD,
                        [(sST, tst)], slow=True)
            tcar = P.emit("vector", lambda e: e.tensor_scalar(
                out=sb_carry[:, :], in0=c128[:, :], scalar1=gamma_ap, scalar2=1.0,
                op0=ALU.mult, op1=ALU.add), [(sLD, tld)], sVE)
            if "dbg_T" in d_dbg:
                P.dma("gpsimd", d_dbg["dbg_T"][:, :], sb_Tinit[:, :], sST, [(sVE, tTi)])
                P.dma("gpsimd", d_dbg["dbg_carry"][:, :], sb_carry[:, :], sST, [(sVE, tcar)])

    p2_phase()
    P.barrier()
    if upto == "p2":
        P.realize()
        return nc, d_dbg

    def p4_phase():
        from contextlib import ExitStack
        with ExitStack() as es:
            xn2s = es.enter_context(nc.sbuf_tensor("p4_xn2", [128, DC * ASB], BF16))
            wsl = es.enter_context(nc.sbuf_tensor("p4_wsl", [128, DC * 1024], BF16))
            wg = es.enter_context(nc.sbuf_tensor("p4_wg", [128, 16 * DC + 1024], BF16))
            eT = es.enter_context(nc.sbuf_tensor("p4_eT", [128, DC * ASB], BF16))
            Ssb = es.enter_context(nc.sbuf_tensor("p4_S", [128, DC * ASB], F32))
            qT = es.enter_context(nc.sbuf_tensor("p4_qT", [128, DC * ASB], BF16))
            qnT = es.enter_context(nc.sbuf_tensor("p4_qnT", [128, H * ASB], BF16))
            ekT = es.enter_context(nc.sbuf_tensor("p4_ekT", [128, H * ASB], BF16))
            ektk = es.enter_context(nc.sbuf_tensor("p4_ektk", [128, (ASB // 128) * 2048], BF16))
            vsb = es.enter_context(nc.sbuf_tensor("p4_vsb", [128, (ASB // 128) * 1024], BF16))
            oT = es.enter_context(nc.sbuf_tensor("p4_oT", [128, DC * ASB], BF16))
            gt = qT
            g1s = es.enter_context(nc.sbuf_tensor("p4_g1", [16, ASB], BF16))
            x1s = es.enter_context(nc.sbuf_tensor("p4_x1s", [128, DC * ASB], F32))
            x2s = x1s
            Asb = es.enter_context(nc.sbuf_tensor("p4_A", [128, 2 * 512], BF16))
            trgt = es.enter_context(nc.sbuf_tensor("p4_trgt", [128, 16 * 128], BF16))
            ps_T = es.enter_context(nc.psum_tensor("p4_psT", [128, 1024], F32))
            ps_A = es.enter_context(nc.psum_tensor("p4_psA", [128, 1024], F32))
            ps_o = es.enter_context(nc.psum_tensor("p4_pso", [128, 1024], F32))
            ps_w = es.enter_context(nc.psum_tensor("p4_psw", [128, 1024], F32))

            ttr = P.dma("sync", AP(trgt, 0, [[16 * 128, 128], [128, 16], [1, 128]]),
                        AP(d_trigtm, 0, [[128, 128], [128 * 128, 16], [1, 128]]), sLD)
            t0 = P.mm(ps_T[:, 0:512], sb_id[:, :], sb_Tinit[:, 0:512], True, True)
            t0 = P.mm(ps_T[:, 512:1024], sb_id[:, :], sb_Tinit[:, 512:1024], True, True)
            snap = {0: P.emit("vector", lambda e: e.tensor_copy(
                out=sb_Tsnap[:, 0:1024], in_=ps_T[:, :]), [(sMM, t0)], sVE)}
            evo_t = None

            for sb in range(MY // ASB):
                col0 = MY + sb * ASB
                ldxn = [P.dma("sync", xn2s[:, ch * ASB:(ch + 1) * ASB],
                              s_xn2[ch * 128:(ch + 1) * 128, col0:col0 + ASB], sLD)
                        for ch in range(DC)]
                wld = [P.dma("gpsimd", wsl[:, dc * 1024:(dc + 1) * 1024],
                             d_wk[dc * 128:(dc + 1) * 128, :], sLG) for dc in range(DC)]
                tvld, tele = [], []
                for c in range(ASB // 128):
                    ti = sb * (ASB // 128) + c
                    dst = AP(ektk, c * 2048, [[(ASB // 128) * 2048, 128], [128, H], [1, 64]])
                    src = AP(s_etok, ti * 128 * D, [[D, 128], [64, H], [1, 64]])
                    tele.append(P.dma("sync", dst, src, sLD))
                    tvld.append(P.dma("sync", vsb[:, c * 1024:(c + 1) * 1024],
                                      AP(s_vtok, ti * 128 * D, [[D, 128], [1, D]]), sLD))
                esc = []
                for ob in range(DC):
                    par = ob % 2
                    wv = [(sLD, ldxn[0]), (sLG, wld[DC - 1])]
                    if ob >= 2:
                        wv.append((sSC, esc[ob - 2]))
                    for dc in range(DC):
                        P.mm(ps_w[:, par * 512:(par + 1) * 512],
                             wsl[:, dc * 1024 + ob * 128:dc * 1024 + (ob + 1) * 128],
                             xn2s[:, dc * ASB:(dc + 1) * ASB],
                             dc == 0, dc == DC - 1,
                             wv if dc == 0 else [(sLD, ldxn[dc])])
                    tmm = P.cnt[sMM]
                    esc.append(P.emit("scalar", lambda e, ob=ob, par=par: e.activation(
                        out=eT[:, ob * ASB:(ob + 1) * ASB],
                        in_=ps_w[:, par * 512:(par + 1) * 512], func=AF.Exp),
                        [(sMM, tmm)], sSC))
                tscan = []
                for ch in range(DC):
                    P.emit("vector", lambda e, ch=ch: e.tensor_tensor_scan(
                        out=Ssb[:, ch * ASB:(ch + 1) * ASB],
                        data0=eT[:, ch * ASB:(ch + 1) * ASB],
                        data1=eT[:, ch * ASB:(ch + 1) * ASB],
                        initial=sb_carry[:, ch:ch + 1],
                        op0=ALU.add, op1=ALU.bypass), [(sSC, esc[ch])], sVE)
                    P.emit("vector", lambda e, ch=ch: e.tensor_copy(
                        out=sb_carry[:, ch:ch + 1],
                        in_=Ssb[:, ch * ASB + ASB - 1:ch * ASB + ASB]), (), sVE)
                    tscan.append(P.emit("vector", lambda e, ch=ch: e.reciprocal(
                        out=Ssb[:, ch * ASB:(ch + 1) * ASB],
                        in_=Ssb[:, ch * ASB:(ch + 1) * ASB]), (), sVE))
                tekT = []
                for h in range(H):
                    ch, hi = h // 2, h % 2
                    w = [(sSC, esc[ch])]
                    if h >= 2:
                        w.append((sVE, tekT[h - 2]))
                    tdm = P.mm(ps_A[:, (h % 2) * 512:(h % 2) * 512 + 512],
                               sb_dup[:, hi * 128:(hi + 1) * 128],
                               eT[:, ch * ASB:(ch + 1) * ASB], True, True, w)
                    if sb == 0 and h == 0 and "dbg_dup" in d_dbg:
                        tcp = P.emit("vector", lambda e: e.tensor_copy(
                            out=Ssb[:, 0:512].bitcast(F32) if False else x1s[:, 0:512],
                            in_=ps_A[:, 0:512]), [(sMM, tdm)], sVE)
                        P.dma("gpsimd", d_dbg["dbg_dup"][:, :], x1s[:, 0:512], sST,
                              [(sVE, tcp)])
                        P.dma("gpsimd", d_dbg["dbg_trigf"][:, :], sb_trigf[:, :], sST)
                        P.dma("gpsimd", d_dbg["dbg_eT"][:, :], eT[:, :], sST,
                              [(sSC, esc[DC - 1])])
                    tekT.append(P.emit("vector", lambda e, h=h, sb=sb: e.tensor_tensor(
                        out=ekT[:, h * ASB:(h + 1) * ASB],
                        in0=ps_A[:, (h % 2) * 512:(h % 2) * 512 + 512],
                        in1=sb_trigf[:, sb * ASB:(sb + 1) * ASB], op=ALU.mult),
                        [(sMM, tdm)], sVE))
                wld2 = [P.dma("gpsimd", wsl[:, dc * 1024:(dc + 1) * 1024],
                              d_wq[dc * 128:(dc + 1) * 128, :], sLG,
                              [(sMM, P.cnt[sMM])]) for dc in range(DC)]
                qsc = []
                for ob in range(DC):
                    par = ob % 2
                    wv = [(sLG, wld2[DC - 1])]
                    wv.append((sSC, esc[DC - 2 + ob] if ob < 2 else qsc[ob - 2]))
                    for dc in range(DC):
                        P.mm(ps_w[:, par * 512:(par + 1) * 512],
                             wsl[:, dc * 1024 + ob * 128:dc * 1024 + (ob + 1) * 128],
                             xn2s[:, dc * ASB:(dc + 1) * ASB],
                             dc == 0, dc == DC - 1, wv if dc == 0 else ())
                    tmm = P.cnt[sMM]
                    qsc.append(P.emit("scalar", lambda e, ob=ob, par=par: e.activation(
                        out=qT[:, ob * ASB:(ob + 1) * ASB],
                        in_=ps_w[:, par * 512:(par + 1) * 512], func=AF.Silu),
                        [(sMM, tmm)], sSC))
                tqr = []
                for ch in range(DC):
                    tqr.append(P.emit("vector", lambda e, ch=ch: e.tensor_tensor(
                        out=qT[:, ch * ASB:(ch + 1) * ASB], in0=qT[:, ch * ASB:(ch + 1) * ASB],
                        in1=Ssb[:, ch * ASB:(ch + 1) * ASB], op=ALU.mult),
                        [(sSC, qsc[ch]), (sVE, tscan[ch])], sVE))
                tqnT = []
                for h in range(H):
                    ch, hi = h // 2, h % 2
                    w = [(sVE, tqr[ch])]
                    if h >= 2:
                        w.append((sVE, tqnT[h - 2]))
                    tdm = P.mm(ps_A[:, (h % 2) * 512:(h % 2) * 512 + 512],
                               sb_dup[:, hi * 128:(hi + 1) * 128],
                               qT[:, ch * ASB:(ch + 1) * ASB], True, True, w)
                    tqnT.append(P.emit("vector", lambda e, h=h, sb=sb: e.tensor_tensor(
                        out=qnT[:, h * ASB:(h + 1) * ASB],
                        in0=ps_A[:, (h % 2) * 512:(h % 2) * 512 + 512],
                        in1=sb_trigf[:, sb * ASB:(sb + 1) * ASB], op=ALU.mult),
                        [(sMM, tdm)], sVE))
                tekt = []
                for c in range(ASB // 128):
                    ek_c = AP(ektk, c * 2048, [[(ASB // 128) * 2048, 128], [128, H], [1, 64]])
                    ek_s = AP(ektk, c * 2048 + 64, [[(ASB // 128) * 2048, 128], [128, H], [1, 64]])
                    cs = AP(trgt, (sb * (ASB // 128) + c) * 128,
                            [[16 * 128, 128], [0, H], [1, 64]])
                    sn = AP(trgt, (sb * (ASB // 128) + c) * 128 + 64,
                            [[16 * 128, 128], [0, H], [1, 64]])
                    P.emit("vector", lambda e, a=ek_s, b=ek_c, c_=sn: e.tensor_tensor(
                        out=a, in0=b, in1=c_, op=ALU.mult),
                        [(sLD, tele[c]), (sLD, ttr)], sVE)
                    tekt.append(P.emit("vector", lambda e, a=ek_c, c_=cs: e.tensor_tensor(
                        out=a, in0=a, in1=c_, op=ALU.mult), (), sVE))

                bank_reader = {0: tqnT[H - 1], 1: tqnT[H - 1]}
                asb_reader = {0: 0, 1: 0}
                for c in range(ASB // 128):
                    k = sb * (ASB // 128) + c
                    wz = [(sVE, snap[k])]
                    if evo_t is not None:
                        wz.append((sVE, evo_t))
                    tz = P.mm(ps_o[:, 0:512], sb_zero_b[:, 0:128], sb_zero_b[:, 0:512],
                              True, True, wz)
                    tz = P.mm(ps_o[:, 512:1024], sb_zero_b[:, 0:128], sb_zero_b[:, 0:512],
                              True, True)
                    first = True
                    for grp in range(4):
                        bk = grp % 2
                        tA = None
                        for gp in range(4):
                            h = grp * 4 + gp
                            w = [(sVE, tqnT[h]), (sVE, tekT[h])]
                            if gp == 0:
                                w.append((sVE, bank_reader[bk]))
                            tA = P.mm(ps_A[:, bk * 512 + gp * 128:bk * 512 + (gp + 1) * 128],
                                      ekT[:, h * ASB + c * 128:h * ASB + (c + 1) * 128],
                                      qnT[:, h * ASB + c * 128:h * ASB + (c + 1) * 128],
                                      True, True, w)
                        tev = P.emit("vector", lambda e, bk=bk: e.tensor_tensor(
                            out=Asb[:, bk * 512:bk * 512 + 512],
                            in0=ps_A[:, bk * 512:bk * 512 + 512],
                            in1=sb_mask[:, :], op=ALU.mult),
                            [(sMM, tA), (sMM, asb_reader[bk])], sVE)
                        bank_reader[bk] = tev
                        for gp in range(4):
                            h = grp * 4 + gp
                            P.mm(ps_o[64 * (h % 2):64 * (h % 2) + 64,
                                      (h // 2) * 128:(h // 2) * 128 + 128],
                                 sb_Tsnap[:, (k % 2) * 1024 + h * 64:(k % 2) * 1024 + (h + 1) * 64],
                                 qnT[:, h * ASB + c * 128:h * ASB + (c + 1) * 128],
                                 False, False, [(sMM, tz)] if first else (),
                                 tile_position=(0, 64 * (h % 2)))
                            first = False
                            P.mm(ps_o[64 * (h % 2):64 * (h % 2) + 64,
                                      (h // 2) * 128:(h // 2) * 128 + 128],
                                 vsb[:, c * 1024 + h * 64:c * 1024 + (h + 1) * 64],
                                 Asb[:, bk * 512 + gp * 128:bk * 512 + (gp + 1) * 128],
                                 False, False,
                                 [(sVE, tev), (sLD, tvld[c])],
                                 tile_position=(0, 64 * (h % 2)))
                        asb_reader[bk] = P.cnt[sMM]
                    tst = None
                    for h in range(H):
                        tst = P.mm(ps_T[:, h * 64:(h + 1) * 64],
                                   AP(ektk, c * 2048 + h * 128,
                                      [[(ASB // 128) * 2048, 128], [1, 128]]),
                                   vsb[:, c * 1024 + h * 64:c * 1024 + (h + 1) * 64],
                                   False, False,
                                   [(sVE, tekt[c]), (sVE, snap[k])] if h == 0 else ())
                    tin = P.cnt[sMM]
                    evo_t = P.emit("vector", lambda e, c=c: e.tensor_copy(
                        out=AP(oT, c * 128, [[DC * ASB, 128], [ASB, DC], [1, 128]]),
                        in_=AP(ps_o, 0, [[1024, 128], [128, DC], [1, 128]])),
                        [(sMM, tin)], sVE)
                    snap[k + 1] = P.emit("vector", lambda e, k=k: e.tensor_copy(
                        out=sb_Tsnap[:, ((k + 1) % 2) * 1024:((k + 1) % 2) * 1024 + 1024],
                        in_=ps_T[:, :]), [(sMM, tst)], sVE)

                dump_t = None
                if sb == 0 and "dbg_oT" in d_dbg:
                    P.dma("gpsimd", d_dbg["dbg_qnT"][:, :], qnT[:, :], sST, [(sVE, tqnT[H - 1])])
                    P.dma("gpsimd", d_dbg["dbg_ekT"][:, :], ekT[:, :], sST, [(sVE, tekT[H - 1])])
                    P.dma("gpsimd", d_dbg["dbg_R"][:, :], Ssb[:, :], sST, [(sVE, tscan[DC - 1])])
                    P.dma("gpsimd", d_dbg["dbg_ektk"][:, :], ektk[:, :], sST, [(sVE, tekt[-1])])
                    dump_t = P.dma("gpsimd", d_dbg["dbg_oT"][:, :], oT[:, :], sST, [(sVE, evo_t)])
                wg1t = P.dma("gpsimd", AP(wg, 0, [[16 * DC + 1024, 128], [16, DC], [1, 16]]),
                             AP(d_wg1, 0, [[H, 128], [128 * H, DC], [1, H]]), sLG)
                wg2t = P.dma("gpsimd", wg[0:16, 16 * DC:16 * DC + 1024], d_wg2[:, :], sLG)
                tg1 = None
                for dc in range(DC):
                    tg1 = P.mm(ps_w[0:16, 0:ASB], wg[:, dc * 16:(dc + 1) * 16],
                               xn2s[:, dc * ASB:(dc + 1) * ASB], dc == 0, dc == DC - 1,
                               [(sLG, wg1t), (sSC, qsc[DC - 2]), (sSC, qsc[DC - 1])]
                               if dc == 0 else ())
                tg1v = P.emit("vector", lambda e: e.tensor_copy(out=g1s[:, :],
                                                                in_=ps_w[0:16, 0:ASB]),
                              [(sMM, tg1)], sVE)
                gsc = []
                for ob in range(DC):
                    par = ob % 2
                    w = [(sLG, wg2t), (sVE, tg1v)]
                    if ob >= 2:
                        w.append((sSC, gsc[ob - 2]))
                    tg2 = P.mm(ps_w[:, par * 512:(par + 1) * 512],
                               wg[0:16, 16 * DC + ob * 128:16 * DC + (ob + 1) * 128],
                               g1s[:, :], True, True, w)
                    gsc.append(P.emit("scalar", lambda e, ob=ob, par=par: e.activation(
                        out=gt[:, ob * ASB:(ob + 1) * ASB],
                        in_=ps_w[:, par * 512:(par + 1) * 512], func=AF.Sigmoid),
                        [(sMM, tg2)], sSC))
                if sb == 0 and "dbg_gate" in d_dbg:
                    P.dma("gpsimd", d_dbg["dbg_gate"][:, :], gt[:, :], sST, [(sSC, gsc[DC - 1])])
                tog, tsq = [], []
                for ch in range(DC):
                    w_og = [(sSC, gsc[ch]), (sVE, evo_t)]
                    if dump_t is not None:
                        w_og.append((sST, dump_t))
                    tog.append(P.emit("vector", lambda e, ch=ch: e.tensor_tensor(
                        out=oT[:, ch * ASB:(ch + 1) * ASB], in0=oT[:, ch * ASB:(ch + 1) * ASB],
                        in1=gt[:, ch * ASB:(ch + 1) * ASB], op=ALU.mult),
                        w_og, sVE))
                for ch in range(DC):
                    tsq.append(P.emit("vector", lambda e, ch=ch: e.tensor_tensor(
                        out=gt[:, ch * ASB:(ch + 1) * ASB], in0=oT[:, ch * ASB:(ch + 1) * ASB],
                        in1=oT[:, ch * ASB:(ch + 1) * ASB], op=ALU.mult), [(sVE, tog[ch])], sVE))
                tms = None
                for ch in range(DC):
                    tms = P.mm(ps_A[0:1, 0:512], sb_ones_b[:, 0:1],
                               gt[:, ch * ASB:(ch + 1) * ASB], ch == 0, ch == DC - 1,
                               [(sVE, tsq[ch])])
                Rr = sb_rows2[0:1, 1024:1024 + ASB]
                P.emit("vector", lambda e: e.tensor_scalar(
                    out=Rr, in0=ps_A[0:1, 0:512], scalar1=1.0 / D, scalar2=RMS_EPS,
                    op0=ALU.mult, op1=ALU.add), [(sMM, tms)], sVE)
                trs = P.emit("scalar", lambda e: e.activation(out=Rr, in_=Rr, func=AF.Sqrt),
                             [(sVE, P.cnt[sVE])], sSC)
                trv = P.emit("vector", lambda e: e.reciprocal(out=Rr, in_=Rr),
                             [(sSC, trs)], sVE)
                tfm = P.mm(ps_A[:, 512:1024], sb_onesrow[0:1, 0:128], Rr, True, True,
                           [(sVE, trv)])
                tno = []
                for ch in range(DC):
                    tv = P.emit("vector", lambda e, ch=ch: e.tensor_tensor(
                        out=oT[:, ch * ASB:(ch + 1) * ASB], in0=oT[:, ch * ASB:(ch + 1) * ASB],
                        in1=ps_A[:, 512:1024], op=ALU.mult), [(sMM, tfm)], sVE)
                    tno.append(P.emit("scalar", lambda e, ch=ch: e.activation(
                        out=oT[:, ch * ASB:(ch + 1) * ASB], in_=oT[:, ch * ASB:(ch + 1) * ASB],
                        func=AF.Identity, bias=0.0, scale=sb_par[:, 120 + ch:121 + ch]),
                        [(sVE, tv)], sSC))
                wld3 = [P.dma("gpsimd", wsl[:, dc * 1024:(dc + 1) * 1024],
                              d_wo[dc * 128:(dc + 1) * 128, :], sLG, [(sMM, tfm)])
                        for dc in range(DC)]
                ldx1 = [P.dma("sync", x1s[:, ch * ASB:(ch + 1) * ASB],
                              s_x1[ch * 128:(ch + 1) * 128, col0:col0 + ASB], sLD)
                        for ch in range(DC)]
                tx2 = [None] * DC
                for ob in range(DC):
                    par = ob % 2
                    wv = [(sLG, wld3[DC - 1]), (sSC, tno[DC - 1])]
                    if ob >= 2:
                        wv.append((sVE, tx2[ob - 2]))
                    if ob < 2:
                        wv.append((sSC, gsc[DC - 2 + ob]))
                    for ic in range(DC):
                        P.mm(ps_w[:, par * 512:(par + 1) * 512],
                             wsl[:, ic * 1024 + ob * 128:ic * 1024 + (ob + 1) * 128],
                             oT[:, ic * ASB:(ic + 1) * ASB], ic == 0, ic == DC - 1,
                             wv if ic == 0 else ())
                    tmm = P.cnt[sMM]
                    tx2[ob] = P.emit("vector", lambda e, ob=ob, par=par: e.tensor_tensor(
                        out=x2s[:, ob * ASB:(ob + 1) * ASB],
                        in0=ps_w[:, par * 512:(par + 1) * 512],
                        in1=x1s[:, ob * ASB:(ob + 1) * ASB], op=ALU.add),
                        [(sMM, tmm), (sLD, ldx1[ob])], sVE)
                for ch in range(DC):
                    P.dma("gpsimd", s_x2[ch * 128:(ch + 1) * 128, sb * ASB:(sb + 1) * ASB],
                          x2s[:, ch * ASB:(ch + 1) * ASB], sST, [(sVE, tx2[ch])])
                    if "dbg_x2" in d_dbg:
                        P.dma("gpsimd",
                              d_dbg["dbg_x2"][ch * 128:(ch + 1) * 128, sb * ASB:(sb + 1) * ASB],
                              x2s[:, ch * ASB:(ch + 1) * ASB], sST, [(sVE, tx2[ch])])
                P.barrier()

    p4_phase()
    if upto == "p4":
        P.realize()
        return nc, d_dbg

    def p5_epilogue(sb, col0, bufs, x1_t):
        x1blk = bufs["x1blk"]
        for ch in range(DC):
            P.dma("gpsimd", d_out[ch * 128:(ch + 1) * 128, col0:col0 + FSB],
                  x1blk[:, ch * FSB:(ch + 1) * FSB], sST, [(sVE, x1_t[ch])])

    ffn_phase(2, MY, s_x2, p5_epilogue)
    P.barrier()
    P.realize()
    return nc, d_dbg



def host_inputs(inputs, core):
    b, half = core // 2, core % 2
    x = np.asarray(inputs["L0_attn_x"])[b]
    xcat = np.concatenate([x[0:MY], x[half * MY:(half + 1) * MY]], axis=0)
    xT = np.ascontiguousarray(xcat.T.astype(np.float32))

    j = np.arange(HD, dtype=np.float64)
    inv = BASE ** (-2.0 * j / HD)
    th_own = (np.arange(half * MY, half * MY + MY, dtype=np.float64)[:, None] * inv[None, :])
    th_pre = (np.arange(0, MY, dtype=np.float64)[:, None] * inv[None, :])
    cos_o, sin_o = np.cos(th_own), np.sin(th_own)
    cos_p, sin_p = np.cos(th_pre), np.sin(th_pre)

    def fc(v):
        v = np.asarray(v, dtype=np.float32)
        return np.ascontiguousarray(v.reshape(-1, 128).T)

    par = np.zeros((128, 129), dtype=np.float32)
    par[:, 0:8] = fc(inputs["L0_ffn_ln1_g"])
    par[:, 8:16] = fc(inputs["L0_ffn_ln1_b"])
    par[:, 16:48] = fc(inputs["L0_ffn_b1"])
    par[:, 48:80] = fc(inputs["L0_ffn_ln2_g"])
    par[:, 80:112] = fc(inputs["L0_ffn_ln2_b"])
    par[:, 112:120] = fc(inputs["L0_ffn_b2"])
    par[:, 120:128] = fc(inputs["L0_attn_norm_w"])
    par[:, 128] = float(half)

    dup0 = np.zeros((128, 128), dtype=np.float32)
    dup1 = np.zeros((128, 128), dtype=np.float32)
    for m in range(128):
        dup0[m % 64, m] = 1.0
        dup1[64 + (m % 64), m] = 1.0

    bf = ml_dtypes.bfloat16
    return {
        "xT": xT,
        "Wq": np.asarray(inputs["L0_attn_Wq"], np.float32),
        "Wk": np.asarray(inputs["L0_attn_Wk"], np.float32),
        "Wv": np.asarray(inputs["L0_attn_Wv"], np.float32),
        "Wo": np.asarray(inputs["L0_attn_Wo"], np.float32),
        "Wg1": np.asarray(inputs["L0_attn_Wg1"], np.float32),
        "Wg2": np.asarray(inputs["L0_attn_Wg2"], np.float32),
        "w1": np.ascontiguousarray(np.asarray(inputs["L0_ffn_w1"], np.float32).T),
        "w2": np.ascontiguousarray(np.asarray(inputs["L0_ffn_w2"], np.float32).T),
        "par": par,
        "b2row": np.asarray(inputs["L0_ffn_b2"], np.float32).reshape(1, D),
        "trigf": np.concatenate([cos_o.T, sin_o.T], 0).astype(np.float32),
        "trigtm": np.concatenate([cos_o, sin_o], 1).astype(bf),
        "trigtp": np.concatenate([cos_p, sin_p], 1).astype(bf),
        "mask": np.triu(np.ones((128, 128), np.float32)).astype(bf),
        "ident": np.eye(128, dtype=np.float32),
        "dup0": dup0.astype(bf),
        "dup1": dup1.astype(bf),
    }


def run(inputs, upto="full", debug=(), cores=None):
    nc, d_dbg = build_program(upto=upto, debug=debug)
    cores = cores if cores is not None else list(range(NCORES))
    in_maps = [host_inputs(inputs, c) for c in cores]
    res = run_bass_kernel_spmd(nc, in_maps, core_ids=cores)
    return res


def kernel(**inputs):
    res = run(inputs)
    out = np.zeros((B, N, D), dtype=np.float32)
    for c in range(NCORES):
        b, half = c // 2, c % 2
        out[b, half * MY:(half + 1) * MY] = res.results[c]["out"].T
    return out


# version 07
# speedup vs baseline: 1.1027x; 1.1027x over previous
import numpy as np
import ml_dtypes

import concourse.bass as bass
import concourse.mybir as mybir
from concourse.bass_utils import run_bass_kernel_spmd

F32 = mybir.dt.float32
BF16 = mybir.dt.bfloat16
AF = mybir.ActivationFunctionType
ALU = mybir.AluOpType
AX = mybir.AxisListType
AP = bass.AP

B, N, D, H, HD = 4, 4096, 1024, 16, 64
MY = N // 2
DC = D // 128
HID = 4096
HC = HID // 128
FSB = 512
ASB = 512
LN_EPS = 1e-5
RMS_EPS = 1e-6
BASE = 10000.0
NCORES = 8

ENGS = ("tensor", "vector", "scalar", "gpsimd", "sync")


class Plan:
    def __init__(self, nc):
        self.nc = nc
        self.steps = {e: [] for e in ENGS}
        self.semnames = []
        self.cnt = {}

    def sem(self, name):
        self.semnames.append(name)
        self.cnt[name] = 0
        return name

    ENG_SEM = {"vector": "sVE", "scalar": "sSC", "gpsimd": "sGP"}

    def emit(self, eng, fn, waits=(), inc=None, k=1, is_barrier=False):
        if not is_barrier and eng in self.ENG_SEM and inc not in ("sLD", "sLG", "sST"):
            sem = self.ENG_SEM[eng]
            if self.cnt[sem] > 0:
                waits = list(waits) + [(sem, self.cnt[sem])]
            inc = sem
            k = 1
        tick = None
        if inc is not None:
            self.cnt[inc] += k
            tick = self.cnt[inc]
        self.steps[eng].append((fn, tuple(waits), inc, k))
        return tick

    def dma(self, eng, out, in_, sem, waits=(), transpose=False, slow=False):
        if transpose:
            fn = lambda e, o=out, i=in_: e.dma_start_transpose(out=o, in_=i)
        else:
            fn = lambda e, o=out, i=in_: e.dma_start(
                out=o, in_=i, allow_slow_non_contiguous=slow)
        return self.emit(eng, fn, waits, sem, 16, is_barrier=True)

    def mm(self, out, lhsT, rhs, start, stop, waits=(), tile_position=None):
        fn = lambda e, o=out, l=lhsT, r=rhs: e.matmul(
            o, l, r, start=start, stop=stop, skip_group_check=True,
            tile_position=tile_position)
        return self.emit("tensor", fn, waits, "sMM")

    def barrier(self):
        for e in ENGS:
            self.emit(e, lambda en: en.nop(), (), "s_" + e, is_barrier=True)
        snap = dict(self.cnt)
        for e in ENGS:
            waits = [(s, snap[s]) for s in self.semnames if snap[s] > 0]
            self.emit(e, lambda en: None, waits, is_barrier=True)

    def realize(self):
        nc = self.nc
        sems = {}
        for n in self.semnames:
            sems[n] = nc.semaphore(n).__enter__()
        with nc.Block() as blk:
            def body(e, eng_name):
                waited = {}
                for fn, waits, inc, k in self.steps[eng_name]:
                    for s, t in waits:
                        if t is None or t <= 0 or waited.get(s, 0) >= t:
                            continue
                        e.wait_ge(sems[s], t)
                        waited[s] = t
                    inst = fn(e)
                    if inc is not None:
                        inst.then_inc(sems[inc], k)

            blk.tensor(lambda e: body(e, "tensor"))
            blk.vector(lambda e: body(e, "vector"))
            blk.scalar(lambda e: body(e, "scalar"))
            blk.gpsimd(lambda e: body(e, "gpsimd"))
            blk.sync(lambda e: body(e, "sync"))


def build_program(upto="full", debug=()):
    debug = set(debug)
    nc = bass.Bass("TRN2", target_bir_lowering=False)
    P = Plan(nc)
    sLD = P.sem("sLD"); sLG = P.sem("sLG"); sST = P.sem("sST")
    sMM = P.sem("sMM"); sVE = P.sem("sVE"); sSC = P.sem("sSC"); sGP = P.sem("sGP")
    for e in ENGS:
        P.sem("s_" + e)

    d_x = nc.dram_tensor("xT", [D, N], F32, kind="ExternalInput")
    d_wq = nc.dram_tensor("Wq", [D, D], F32, kind="ExternalInput")
    d_wk = nc.dram_tensor("Wk", [D, D], F32, kind="ExternalInput")
    d_wv = nc.dram_tensor("Wv", [D, D], F32, kind="ExternalInput")
    d_wo = nc.dram_tensor("Wo", [D, D], F32, kind="ExternalInput")
    d_wg1 = nc.dram_tensor("Wg1", [D, H], F32, kind="ExternalInput")
    d_wg2 = nc.dram_tensor("Wg2", [H, D], F32, kind="ExternalInput")
    d_w1 = nc.dram_tensor("w1", [D, HID], F32, kind="ExternalInput")
    d_w2 = nc.dram_tensor("w2", [HID, D], F32, kind="ExternalInput")
    d_par = nc.dram_tensor("par", [128, 129], F32, kind="ExternalInput")
    d_b2r = nc.dram_tensor("b2row", [1, D], F32, kind="ExternalInput")
    d_trigf = nc.dram_tensor("trigf", [128, MY], F32, kind="ExternalInput")
    d_trigtm = nc.dram_tensor("trigtm", [MY, 128], BF16, kind="ExternalInput")
    d_trigtp = nc.dram_tensor("trigtp", [MY, 128], BF16, kind="ExternalInput")
    d_mask = nc.dram_tensor("mask", [128, 128], BF16, kind="ExternalInput")
    d_ident = nc.dram_tensor("ident", [128, 128], F32, kind="ExternalInput")
    d_dup0 = nc.dram_tensor("dup0", [128, 128], BF16, kind="ExternalInput")
    d_dup1 = nc.dram_tensor("dup1", [128, 128], BF16, kind="ExternalInput")

    d_out = nc.dram_tensor("out", [D, MY], F32, kind="ExternalOutput")
    d_dbg = {}

    def dbg_out(name, shape, dt):
        d_dbg[name] = nc.dram_tensor(name, shape, dt, kind="ExternalOutput")
        return d_dbg[name]

    if "qz" in debug:
        dbg_out("dbg_sc", [128, 8], F32)
        dbg_out("dbg_uwb", [1, 3 * D], BF16)
        dbg_out("dbg_q1s", [128, HID], BF16)
        dbg_out("dbg_q2s", [128, D], BF16)
    if "lnin" in debug:
        dbg_out("dbg_xn", [128, DC * FSB], BF16)
        dbg_out("dbg_h", [128, FSB], BF16)
        dbg_out("dbg_h2", [128, FSB], BF16)
    if "x1" in debug:
        dbg_out("dbg_x1", [D, N], F32)
    if "xn2" in debug:
        dbg_out("dbg_xn2", [D, N], BF16)
    if "ev" in debug:
        dbg_out("dbg_e", [MY // 128, 128, D], BF16)
        dbg_out("dbg_v", [MY // 128, 128, D], BF16)
        dbg_out("dbg_T", [128, 1024], F32)
        dbg_out("dbg_carry", [128, DC], F32)
    if "x2" in debug:
        dbg_out("dbg_x2", [D, MY], F32)
    if "p4s" in debug:
        dbg_out("dbg_oT", [128, DC * ASB], BF16)
        dbg_out("dbg_qnT", [128, H * ASB], BF16)
        dbg_out("dbg_ekT", [128, H * ASB], BF16)
        dbg_out("dbg_R", [128, DC * ASB], F32)
        dbg_out("dbg_ektk", [128, (ASB // 128) * 2048], BF16)
        dbg_out("dbg_gate", [128, DC * ASB], BF16)
        dbg_out("dbg_trigf", [128, MY], F32)
        dbg_out("dbg_dup", [128, 512], F32)
        dbg_out("dbg_eT", [128, DC * ASB], BF16)


    s_q1 = nc.dram_tensor("q1d", [D, HID], BF16)
    s_q2 = nc.dram_tensor("q2d", [HID, D], BF16)
    s_x1 = nc.dram_tensor("x1d", [D, N], F32)
    s_xn2 = nc.dram_tensor("xn2d", [D, N], BF16)
    s_xn2t = nc.dram_tensor("xn2td", [N // 128, 128, D], BF16)
    s_etok = nc.dram_tensor("etokd", [MY // 128, 128, D], BF16)
    s_vtok = nc.dram_tensor("vtokd", [MY // 128, 128, D], BF16)
    s_x2 = nc.dram_tensor("x2d", [D, MY], F32)
    s_crow = nc.dram_tensor("crowd", [1, D], F32)

    alloc = nc.alloc_sbuf_tensor
    sb_par = alloc("sb_par", [128, 129], F32)
    sb_b2r = alloc("sb_b2r", [1, D], F32)
    sb_mask = alloc("sb_mask", [128, 512], BF16)
    sb_id = alloc("sb_id", [128, 128], F32)
    sb_dup = alloc("sb_dup", [128, 256], BF16)
    sb_ones_b = alloc("sb_ones_b", [128, 1], BF16)
    sb_ones_f = alloc("sb_ones_f", [128, 1], F32)
    sb_onesrow = alloc("sb_onesrow", [1, 512], F32)
    sb_zero_b = alloc("sb_zero_b", [128, 512], BF16)
    sb_rows = alloc("sb_rows", [1, 2048], F32)
    sb_rows2 = alloc("sb_rows2", [1, 1536], F32)
    sb_sc = alloc("sb_sc", [128, 8], F32)
    sb_carry = alloc("sb_carry", [128, DC], F32)
    sb_Tinit = alloc("sb_Tinit", [128, 1024], F32)
    sb_Tsnap = alloc("sb_Tsnap", [128, 2048], BF16)
    sb_trigf = alloc("sb_trigf", [128, MY], F32)
    gamma_ap = sb_par[:, 128:129]

    P.dma("sync", sb_par[:, :], d_par[:, :], sLD)
    P.dma("sync", sb_b2r[:, :], d_b2r[:, :], sLD)
    for r in range(4):
        P.dma("sync", sb_mask[:, r * 128:(r + 1) * 128], d_mask[:, :], sLD)
    P.dma("sync", sb_id[:, :], d_ident[:, :], sLD)
    P.dma("sync", sb_dup[:, 0:128], d_dup0[:, :], sLD)
    P.dma("sync", sb_dup[:, 128:256], d_dup1[:, :], sLD)
    P.dma("sync", sb_trigf[:, :], d_trigf[:, :], sLD)
    for fn in (lambda e: e.memset(sb_ones_b[:, :], 1.0),
               lambda e: e.memset(sb_ones_f[:, :], 1.0),
               lambda e: e.memset(sb_onesrow[:, :], 1.0),
               lambda e: e.memset(sb_zero_b[:, :], 0.0)):
        P.emit("vector", fn, (), sVE)
    P.barrier()

    sb_uwb = alloc("sb_uwb", [1, 3 * D], BF16)

    def quantize(widx, d_w, s_q, nslab, ncols, scol):
        from contextlib import ExitStack
        with ExitStack() as es:
            slab = es.enter_context(nc.sbuf_tensor(f"qz_slab{widx}", [128, 2 * ncols], F32))
            work = es.enter_context(nc.sbuf_tensor(f"qz_work{widx}", [128, 2 * ncols], F32))
            acc = es.enter_context(nc.sbuf_tensor(f"qz_acc{widx}", [128, nslab + 1], F32))
            qt = es.enter_context(nc.sbuf_tensor(f"qz_q{widx}", [128, 2 * ncols], BF16))
            row = es.enter_context(nc.sbuf_tensor(f"qz_row{widx}", [1, 512], F32))
            accrow = es.enter_context(nc.sbuf_tensor(f"qz_ar{widx}", [1, 128], F32))
            ps = es.enter_context(nc.psum_tensor(f"qz_ps{widx}", [128, 512], F32))
            if widx == 1:
                uwl = es.enter_context(nc.sbuf_tensor("qz_uwl", [128, 2], BF16))
                psu = es.enter_context(nc.psum_tensor("qz_psu", [1, 1024], F32))
                psw = es.enter_context(nc.psum_tensor("qz_psw", [2, 1024], F32))

            ve_t = [None] * nslab
            for s in range(nslab):
                half = (s % 2) * ncols
                w = [] if s < 2 else [(sVE, ve_t[s - 2])]
                tl = P.dma("sync", slab[:, half:half + ncols],
                           d_w[s * 128:(s + 1) * 128, :], sLD, w)
                ve_t[s] = P.emit("vector", lambda e, s=s, half=half: e.tensor_reduce(
                    out=acc[:, s:s + 1], in_=slab[:, half:half + ncols],
                    axis=AX.X, op=ALU.add, apply_absolute_value=True),
                    [(sLD, tl)], sVE)
            tv = P.emit("vector", lambda e: e.tensor_reduce(
                out=acc[:, nslab:nslab + 1], in_=acc[:, 0:nslab], axis=AX.X,
                op=ALU.add), (), sVE)
            t1 = P.dma("gpsimd", AP(s_crow, 0, [[1, 128], [1, 1]]),
                       acc[:, nslab:nslab + 1], sST, [(sVE, tv)])
            t2 = P.dma("sync", accrow[0:1, :], s_crow[0:1, 0:128], sLD, [(sST, t1)])
            tv = P.emit("vector", lambda e: e.tensor_reduce(
                out=row[0:1, 0:1], in_=accrow[0:1, :], axis=AX.X, op=ALU.add),
                [(sLD, t2)], sVE)
            tv = P.emit("vector", lambda e: e.tensor_scalar(
                out=row[0:1, 0:1], in0=row[0:1, 0:1],
                scalar1=1.0 / (128.0 * nslab * ncols), scalar2=1e-5,
                op0=ALU.mult, op1=ALU.max), (), sVE)
            tv = P.emit("vector", lambda e: e.tensor_scalar(
                out=row[0:1, 1:2], in0=row[0:1, 0:1], scalar1=0.5, scalar2=None,
                op0=ALU.mult), (), sVE)
            tm = P.mm(ps[:, 0:512], sb_onesrow[0:1, 0:128], row[0:1, 0:512], True, True,
                      [(sVE, tv)])
            tv = P.emit("vector", lambda e: e.tensor_copy(
                out=sb_sc[:, scol:scol + 2], in_=ps[:, 0:2]), [(sMM, tm)], sVE)

            st_t = [None] * nslab
            qt_t = [None] * nslab
            for s in range(nslab):
                half = (s % 2) * ncols
                w = [(sVE, tv)] if s < 2 else [(sST, st_t[s - 2])]
                tl = P.dma("sync", slab[:, half:half + ncols],
                           d_w[s * 128:(s + 1) * 128, :], sLD, w)
                wsg = [(sLD, tl)]
                if s >= 2:
                    wsg.append((sVE, qt_t[s - 2]))
                tsg = P.emit("scalar", lambda e, half=half: e.sign(
                    out=work[:, half:half + ncols], in_=slab[:, half:half + ncols]),
                    wsg, sSC)
                tab = P.emit("scalar", lambda e, half=half: e.activation(
                    out=slab[:, half:half + ncols], in_=slab[:, half:half + ncols],
                    func=AF.Abs), (), sSC)
                P.emit("vector", lambda e, half=half: e.tensor_single_scalar(
                    out=slab[:, half:half + ncols], in_=slab[:, half:half + ncols],
                    scalar=sb_sc[:, scol + 1:scol + 2], op=ALU.is_gt),
                    [(sSC, tab)], sVE)
                wq = [] if s < 2 else [(sST, st_t[s - 2])]
                tq = P.emit("vector", lambda e, half=half: e.tensor_tensor(
                    out=qt[:, half:half + ncols], in0=slab[:, half:half + ncols],
                    in1=work[:, half:half + ncols], op=ALU.mult), wq, sVE)
                if widx == 1:
                    tl1 = P.emit("vector", lambda e: e.tensor_copy(
                        out=uwl[:, 0:1], in_=sb_ones_b[:, 0:1]), [(sVE, tq)], sVE)
                    tl1 = P.emit("vector", lambda e, s=s: e.tensor_copy(
                        out=uwl[:, 1:2], in_=sb_par[:, 80 + s:81 + s]), (), sVE)
                    tmw = None
                    for tq_ in range(2):
                        tmw = P.mm(psw[0:2, tq_ * 512:(tq_ + 1) * 512], uwl[:, 0:2],
                                   qt[:, half + tq_ * 512:half + (tq_ + 1) * 512],
                                   s == 0, s == nslab - 1, [(sVE, tl1)])
                    tq = P.emit("vector", lambda e, half=half, s=s: e.tensor_scalar(
                        out=qt[:, half:half + ncols], in0=qt[:, half:half + ncols],
                        scalar1=sb_par[:, 48 + s:49 + s], scalar2=None, op0=ALU.mult),
                        [(sMM, tmw)], sVE)
                    for tq_ in range(2):
                        P.mm(psu[0:1, tq_ * 512:(tq_ + 1) * 512], sb_ones_b[:, 0:1],
                             qt[:, half + tq_ * 512:half + (tq_ + 1) * 512],
                             s == 0, s == nslab - 1, [(sVE, tq)])
                qt_t[s] = tq
                st_t[s] = P.dma("gpsimd", s_q[s * 128:(s + 1) * 128, :],
                                qt[:, half:half + ncols], sST, [(sVE, tq)])
            if widx == 1:
                tmm_all = P.cnt[sMM]
                t1 = P.emit("vector", lambda e: e.tensor_copy(
                    out=sb_uwb[0:1, 0:D], in_=psu[0:1, :]), [(sMM, tmm_all)], sVE)
                t2 = P.emit("vector", lambda e: e.tensor_copy(
                    out=work[0:2, 0:D], in_=psw[0:2, :]), (), sVE)
                P.dma("gpsimd", sb_uwb[0:1, D:2 * D], work[1:2, 0:D], sST, [(sVE, t2)])
                P.dma("gpsimd", sb_uwb[0:1, 2 * D:3 * D], d_b2r[0:1, :], sST)

    quantize(0, d_w1, s_q1, DC, HID, 0)
    P.barrier()
    quantize(1, d_w2, s_q2, HC, D, 2)
    P.emit("vector", lambda e: e.tensor_scalar(
        out=sb_sc[:, 4:5], in0=sb_sc[:, 2:3], scalar1=0.5, scalar2=None,
        op0=ALU.mult), (), sVE)
    P.emit("vector", lambda e: e.reciprocal(out=sb_sc[:, 5:6], in_=sb_sc[:, 2:3]),
           (), sVE)
    P.barrier()

    if "dbg_sc" in d_dbg:
        P.dma("gpsimd", d_dbg["dbg_sc"][:, :], sb_sc[:, :], sST)
        P.dma("gpsimd", d_dbg["dbg_uwb"][:, :], sb_uwb[0:1, :], sST)
        P.dma("gpsimd", d_dbg["dbg_q1s"][:, :], s_q1[0:128, :], sST)
        P.dma("gpsimd", d_dbg["dbg_q2s"][:, :], s_q2[0:128, :], sST)
        P.barrier()
    if upto == "p0":
        P.realize()
        return nc, d_dbg

    def ffn_phase(tag, tokens, in_dram, epilogue):
        from contextlib import ExitStack
        with ExitStack() as es:
            xblk = es.enter_context(nc.sbuf_tensor(f"fx{tag}", [128, DC * FSB], F32))
            xbf = es.enter_context(nc.sbuf_tensor(f"fxb{tag}", [128, DC * FSB], BF16))
            xn = es.enter_context(nc.sbuf_tensor(f"fxn{tag}", [128, DC * FSB], BF16))
            hbuf = es.enter_context(nc.sbuf_tensor(f"fh{tag}", [128, HC * FSB], BF16))
            sq = es.enter_context(nc.sbuf_tensor(f"fsq{tag}", [128, 2 * FSB], BF16))
            f32s = es.enter_context(nc.sbuf_tensor(f"f32s{tag}", [128, FSB], F32))
            rowsb = es.enter_context(nc.sbuf_tensor(f"frb{tag}", [1, 1536], BF16))
            cfA = es.enter_context(nc.sbuf_tensor(f"fcA{tag}", [128, FSB], F32))
            cfB = es.enter_context(nc.sbuf_tensor(f"fcB{tag}", [128, FSB], F32))
            wt = es.enter_context(nc.sbuf_tensor(f"fwt{tag}", [128, 2 * HID], BF16))
            x1blk = es.enter_context(nc.sbuf_tensor(f"fx1{tag}", [128, DC * FSB], F32))
            ps = es.enter_context(nc.psum_tensor(f"fps{tag}", [128, 2048], F32))
            pr = [es.enter_context(nc.psum_tensor(f"fpr{tag}{i}", [1, 512], F32))
                  for i in range(4)]
            bufs = dict(xblk=xblk, xbf=xbf, xn=xn, hbuf=hbuf, sq=sq, cfA=cfA,
                        cfB=cfB, wt=wt, x1blk=x1blk, ps=ps, pr=pr, f32s=f32s, rowsb=rowsb)
            for sb in range(tokens // FSB):
                _ffn_sb(tag, sb, sb * FSB, in_dram, epilogue, bufs)
                P.barrier()

    def ln_stats(src_f32, src_w, dst_bf, dst_sq, pr, width):
        gp_t, sc_t = [], []
        for ch in range(DC):
            gp_t.append(P.emit("gpsimd", lambda e, ch=ch: e.tensor_copy(
                out=dst_bf[:, ch * width:(ch + 1) * width],
                in_=src_f32[:, ch * width:(ch + 1) * width]),
                [(src_w[0], src_w[1][ch])] if src_w else (), sGP))
        for ch in range(DC):
            sc_t.append(P.emit("scalar", lambda e, ch=ch: e.activation(
                out=dst_sq[:, ch * width:(ch + 1) * width],
                in_=dst_bf[:, ch * width:(ch + 1) * width], func=AF.Square),
                [(sGP, gp_t[ch])], sSC))
        tm = None
        for ch in range(DC):
            for tq in range(width // 512):
                c = ch * width + tq * 512
                tm = P.mm(pr[tq][0:1, :], sb_ones_b[:, 0:1], dst_bf[:, c:c + 512],
                          ch == 0, ch == DC - 1, [(sGP, gp_t[ch])])
                tm = P.mm(pr[2 + tq][0:1, :], sb_ones_b[:, 0:1], dst_sq[:, c:c + 512],
                          ch == 0, ch == DC - 1, [(sSC, sc_t[ch])])
        return tm

    def ln_rows(pr, Rm, Rv, Ra, Rb, width, denom, fused, tm_wait):
        for tq in range(width // 512):
            P.emit("vector", lambda e, tq=tq: e.tensor_scalar(
                out=Rm[0:1, tq * 512:(tq + 1) * 512], in0=pr[tq][0:1, :],
                scalar1=1.0 / denom, scalar2=None, op0=ALU.mult), [(sMM, tm_wait)], sVE)
            P.emit("vector", lambda e, tq=tq: e.tensor_scalar(
                out=Rv[0:1, tq * 512:(tq + 1) * 512], in0=pr[2 + tq][0:1, :],
                scalar1=1.0 / denom, scalar2=None, op0=ALU.mult), [(sMM, tm_wait)], sVE)
        P.emit("vector", lambda e: e.tensor_tensor(out=Ra, in0=Rm, in1=Rm, op=ALU.mult), (), sVE)
        P.emit("vector", lambda e: e.tensor_tensor(out=Rv, in0=Rv, in1=Ra, op=ALU.subtract), (), sVE)
        P.emit("vector", lambda e: e.tensor_scalar(out=Ra, in0=Rv, scalar1=LN_EPS,
                                                   scalar2=None, op0=ALU.add), (), sVE)
        if fused:
            P.emit("vector", lambda e: e.reciprocal(out=Rb, in_=Ra), (), sVE)
            P.emit("vector", lambda e: e.tensor_tensor(out=Rb, in0=Rv, in1=Rb, op=ALU.mult), (), sVE)
            P.emit("vector", lambda e: e.tensor_scalar(out=Rb, in0=Rb, scalar1=LN_EPS,
                                                       scalar2=None, op0=ALU.add), (), sVE)
            P.emit("vector", lambda e: e.tensor_tensor(out=Ra, in0=Ra, in1=Rb, op=ALU.mult), (), sVE)
        tsc = P.emit("scalar", lambda e: e.activation(out=Ra, in_=Ra, func=AF.Sqrt),
                     [(sVE, P.cnt[sVE])], sSC)
        P.emit("vector", lambda e: e.reciprocal(out=Ra, in_=Ra), [(sSC, tsc)], sVE)
        P.emit("vector", lambda e: e.tensor_tensor(out=Rb, in0=Rm, in1=Ra, op=ALU.mult), (), sVE)
        tv = P.emit("vector", lambda e: e.tensor_scalar(out=Rb, in0=Rb, scalar1=-1.0,
                                                        scalar2=None, op0=ALU.mult), (), sVE)
        return tv

    def bcast_fields(Ra, Rb, cfA, cfB, width, ps, wv):
        last = None
        for tq in range(width // 512):
            t1 = P.mm(ps[:, tq * 512:(tq + 1) * 512], sb_onesrow[0:1, 0:128],
                      Ra[0:1, tq * 512:(tq + 1) * 512], True, True, [(sVE, wv)])
            t2 = P.mm(ps[:, 1024 + tq * 512:1024 + (tq + 1) * 512], sb_onesrow[0:1, 0:128],
                      Rb[0:1, tq * 512:(tq + 1) * 512], True, True, [(sVE, wv)])
            P.emit("scalar", lambda e, tq=tq: e.copy(
                out=cfA[:, tq * 512:(tq + 1) * 512], in_=ps[:, tq * 512:(tq + 1) * 512]),
                [(sMM, t1)], sSC)
            last = P.emit("scalar", lambda e, tq=tq: e.copy(
                out=cfB[:, tq * 512:(tq + 1) * 512],
                in_=ps[:, 1024 + tq * 512:1024 + (tq + 1) * 512]), [(sMM, t2)], sSC)
        return last

    def _ffn_sb(tag, sb, col0, in_dram, epilogue, bufs):
        xblk, xbf, xn, hbuf, sq = bufs["xblk"], bufs["xbf"], bufs["xn"], bufs["hbuf"], bufs["sq"]
        cfA, cfB, wt, x1blk, ps, pr = (bufs["cfA"], bufs["cfB"], bufs["wt"],
                                       bufs["x1blk"], bufs["ps"], bufs["pr"])
        f32s = bufs["f32s"]
        rowsb = bufs["rowsb"]
        ld_t = []
        for ch in range(DC):
            ld_t.append(P.dma("sync", xblk[:, ch * FSB:(ch + 1) * FSB],
                              in_dram[ch * 128:(ch + 1) * 128, col0:col0 + FSB], sLD))
        tm = ln_stats(xblk, (sLD, ld_t), xbf, xn, pr, FSB)
        R = lambda i: sb_rows[0:1, i * FSB:(i + 1) * FSB]
        tv = ln_rows(pr, R(0), R(1), R(2), R(3), FSB, D, True, tm)
        tf = bcast_fields(R(2), R(3), cfA, cfB, FSB, ps, tv)
        sc_xn = []
        for ch in range(DC):
            wv1 = [(sSC, tf)]
            if ch >= 1:
                wv1.append((sSC, sc_xn[ch - 1]))
            P.emit("vector", lambda e, ch=ch: e.tensor_tensor(
                out=f32s[:, :], in0=xblk[:, ch * FSB:(ch + 1) * FSB],
                in1=cfA[:, :], op=ALU.mult), wv1, sVE)
            t = P.emit("vector", lambda e, ch=ch: e.tensor_tensor(
                out=f32s[:, :], in0=f32s[:, :],
                in1=cfB[:, :], op=ALU.add), (), sVE)
            t = P.emit("scalar", lambda e, ch=ch: e.activation(
                out=xn[:, ch * FSB:(ch + 1) * FSB], in_=f32s[:, :],
                func=AF.Identity, bias=sb_par[:, 8 + ch:9 + ch],
                scale=sb_par[:, ch:ch + 1]), [(sVE, t)], sSC)
            sc_xn.append(t)

        if tag == 1 and sb == 0 and "dbg_xn" in d_dbg:
            P.dma("gpsimd", d_dbg["dbg_xn"][:, :], xn[:, :], sST, [(sSC, sc_xn[DC - 1])])
        evac_t = {}
        sq_mm = {}
        wslab_t = {}
        for hb in range(HC):
            if hb % 2 == 0:
                hbp = hb // 2
                w = [] if hbp < 2 else [(sSC, evac_t[hb - 3])]
                for hbi in range(2):
                    wslab_t[hbp] = P.dma(
                        "sync",
                        AP(wt, (hbp % 2) * 2048 + hbi * 1024,
                           [[2 * HID, 128], [128, DC], [1, 128]]),
                        AP(s_q1, (hb + hbi) * 128, [[HID, 128], [128 * HID, DC], [1, 128]]),
                        sLD, w)
            par = hb % 2
            hbp = hb // 2
            pcol = par * 1024
            wv0 = [(sLD, wslab_t[hbp]), (sSC, sc_xn[0])]
            if hb >= 2:
                wv0.append((sSC, evac_t[hb - 2]))
            for dc in range(DC):
                lhs = wt[:, (hbp % 2) * 2048 + (hb % 2) * 1024 + dc * 128:
                         (hbp % 2) * 2048 + (hb % 2) * 1024 + (dc + 1) * 128]
                wv = wv0 if dc == 0 else [(sSC, sc_xn[dc])]
                for tq in range(FSB // 512):
                    P.mm(ps[:, pcol + tq * 512:pcol + (tq + 1) * 512], lhs,
                         xn[:, dc * FSB + tq * 512:dc * FSB + (tq + 1) * 512],
                         dc == 0, dc == DC - 1, wv)
            tmm = P.cnt[sMM]
            tev = P.emit("scalar", lambda e, hb=hb, pcol=pcol: e.activation(
                out=hbuf[:, hb * FSB:(hb + 1) * FSB], in_=ps[:, pcol:pcol + FSB],
                func=AF.Gelu, bias=sb_par[:, 16 + hb:17 + hb], scale=sb_sc[:, 0:1]),
                [(sMM, tmm)], sSC)
            evac_t[hb] = tev
            if tag == 1 and sb == 0 and hb == 0 and "dbg_h" in d_dbg:
                P.dma("gpsimd", d_dbg["dbg_h"][:, :], hbuf[:, 0:FSB], sST, [(sSC, tev)])
            wsq = [(sSC, tev)]
            if hb >= 2:
                wsq.append((sMM, sq_mm[hb - 2]))
            tsq = P.emit("vector", lambda e, hb=hb: e.tensor_tensor(
                out=sq[:, (hb % 2) * FSB:(hb % 2 + 1) * FSB],
                in0=hbuf[:, hb * FSB:(hb + 1) * FSB],
                in1=hbuf[:, hb * FSB:(hb + 1) * FSB], op=ALU.mult), wsq, sVE)
            P.mm(pr[0][0:1, :], sb_ones_b[:, 0:1],
                 hbuf[:, hb * FSB:hb * FSB + 512],
                 hb == 0, hb == HC - 1, [(sSC, tev)])
            sq_mm[hb] = P.mm(pr[2][0:1, :], sb_ones_b[:, 0:1],
                             sq[:, (hb % 2) * FSB:(hb % 2) * FSB + 512],
                             hb == 0, hb == HC - 1, [(sVE, tsq)])
        tm2 = P.cnt[sMM]
        Rmu = sb_rows2[0:1, 0:512]
        Rv = sb_rows2[0:1, 512:1024]
        Rinv = sb_rows[0:1, 0:512]
        RF = sb_rows[0:1, 512:1024]
        Rtmp = sb_rows[0:1, 1024:1536]
        P.emit("vector", lambda e: e.tensor_scalar(
            out=Rmu, in0=pr[0][0:1, :], scalar1=-1.0 / HID, scalar2=None,
            op0=ALU.mult), [(sMM, tm2)], sVE)
        P.emit("vector", lambda e: e.tensor_scalar(
            out=Rv, in0=pr[2][0:1, :], scalar1=1.0 / HID, scalar2=None,
            op0=ALU.mult), [(sMM, tm2)], sVE)
        P.emit("vector", lambda e: e.tensor_tensor(out=Rtmp, in0=Rmu, in1=Rmu,
                                                   op=ALU.mult), (), sVE)
        P.emit("vector", lambda e: e.tensor_tensor(out=Rv, in0=Rv, in1=Rtmp,
                                                   op=ALU.subtract), (), sVE)
        tv = P.emit("vector", lambda e: e.tensor_scalar(
            out=Rv, in0=Rv, scalar1=LN_EPS, scalar2=None, op0=ALU.add), (), sVE)
        tsc = P.emit("scalar", lambda e: e.activation(out=Rinv, in_=Rv, func=AF.Sqrt),
                     [(sVE, tv)], sSC)
        P.emit("vector", lambda e: e.reciprocal(out=RF, in_=Rinv), [(sSC, tsc)], sVE)
        P.emit("vector", lambda e: e.tensor_scalar(
            out=Rtmp, in0=Rinv, scalar1=sb_sc[0:1, 5:6], scalar2=None,
            op0=ALU.mult), (), sVE)
        trf = P.emit("vector", lambda e: e.tensor_scalar(
            out=RF, in0=RF, scalar1=sb_sc[0:1, 4:5], scalar2=None,
            op0=ALU.mult), (), sVE)
        P.emit("vector", lambda e: e.tensor_copy(out=rowsb[0:1, 0:512], in_=Rmu), (), sVE)
        P.emit("vector", lambda e: e.tensor_copy(out=rowsb[0:1, 512:1024], in_=Rinv), (), sVE)
        trow = P.emit("vector", lambda e: e.tensor_copy(out=rowsb[0:1, 1024:1536],
                                                        in_=Rtmp), (), sVE)
        tfm = P.mm(ps[:, 512:1024], sb_onesrow[0:1, 0:128], RF, True, True,
                   [(sVE, trf)])
        tFc = P.emit("scalar", lambda e: e.copy(out=cfA[:, :], in_=ps[:, 512:1024]),
                     [(sMM, tfm), (sSC, sc_xn[DC - 1])], sSC)
        x1_t = [None] * DC
        wslab2_t = {}
        for db in range(DC):
            par = db % 2
            pcol = par * 1024
            w = [(sSC, evac_t[HC - 1])] if db < 2 else [(sVE, x1_t[db - 2])]
            wslab2_t[db] = P.dma("sync",
                                 AP(wt, (db % 2) * HID, [[2 * HID, 128], [128, HC], [1, 128]]),
                                 AP(s_q2, db * 128, [[D, 128], [128 * D, HC], [1, 128]]),
                                 sLD, w)
            for hc in range(HC):
                lhs = wt[:, par * HID + hc * 128:par * HID + (hc + 1) * 128]
                wv = [(sLD, wslab2_t[db]), (sSC, evac_t[hc])]
                if hc == 0 and db < 2:
                    wv.append((sSC, evac_t[HC - 1]))
                if hc == 0 and db >= 2:
                    wv.append((sVE, x1_t[db - 2]))
                P.mm(ps[:, pcol:pcol + 512], lhs,
                     hbuf[:, hc * FSB:hc * FSB + 512],
                     hc == 0, False, wv)
            for ki in range(3):
                P.mm(ps[:, pcol:pcol + 512],
                     sb_uwb[0:1, ki * D + db * 128:ki * D + (db + 1) * 128],
                     rowsb[0:1, ki * 512:(ki + 1) * 512], False, ki == 2,
                     [(sVE, trow)])
            tmm = P.cnt[sMM]
            tv1 = P.emit("vector", lambda e, db=db, pcol=pcol: e.tensor_tensor(
                out=f32s[:, :], in0=ps[:, pcol:pcol + 512], in1=cfA[:, :],
                op=ALU.mult), [(sMM, tmm), (sSC, tFc)], sVE)
            x1_t[db] = P.emit("vector", lambda e, db=db: e.tensor_tensor(
                out=x1blk[:, db * FSB:(db + 1) * FSB], in0=f32s[:, :],
                in1=xblk[:, db * FSB:(db + 1) * FSB], op=ALU.add), (), sVE)
        epilogue(sb, col0, bufs, x1_t)

    def p1_epilogue(sb, col0, bufs, x1_t):
        xbf, xn, x1blk, ps, pr, cfA, cfB = (bufs["xbf"], bufs["xn"], bufs["x1blk"],
                                            bufs["ps"], bufs["pr"], bufs["cfA"], bufs["cfB"])
        for ch in range(DC):
            P.dma("gpsimd", s_x1[ch * 128:(ch + 1) * 128, col0:col0 + FSB],
                  x1blk[:, ch * FSB:(ch + 1) * FSB], sST, [(sVE, x1_t[ch])])
            if "dbg_x1" in d_dbg:
                P.dma("gpsimd", d_dbg["dbg_x1"][ch * 128:(ch + 1) * 128, col0:col0 + FSB],
                      x1blk[:, ch * FSB:(ch + 1) * FSB], sST, [(sVE, x1_t[ch])])
        tm = ln_stats(x1blk, (sVE, x1_t), xbf, xn, pr, FSB)
        R = lambda i: sb_rows[0:1, i * FSB:(i + 1) * FSB]
        tv = ln_rows(pr, R(0), R(1), R(2), R(3), FSB, D, False, tm)
        tf = bcast_fields(R(2), R(3), cfA, cfB, FSB, ps, tv)
        f32s = bufs["f32s"]
        ve_t = []
        for ch in range(DC):
            P.emit("vector", lambda e, ch=ch: e.tensor_tensor(
                out=f32s[:, :], in0=x1blk[:, ch * FSB:(ch + 1) * FSB],
                in1=cfA[:, :], op=ALU.mult), [(sSC, tf)], sVE)
            ve_t.append(P.emit("vector", lambda e, ch=ch: e.tensor_tensor(
                out=xn[:, ch * FSB:(ch + 1) * FSB], in0=f32s[:, :],
                in1=cfB[:, :], op=ALU.add), (), sVE))
        for ch in range(DC):
            P.dma("gpsimd", s_xn2[ch * 128:(ch + 1) * 128, col0:col0 + FSB],
                  xn[:, ch * FSB:(ch + 1) * FSB], sST, [(sVE, ve_t[ch])])
            if "dbg_xn2" in d_dbg:
                P.dma("gpsimd", d_dbg["dbg_xn2"][ch * 128:(ch + 1) * 128, col0:col0 + FSB],
                      xn[:, ch * FSB:(ch + 1) * FSB], sST, [(sVE, ve_t[ch])])
        for tt in range(FSB // 128):
            tile = (col0 + tt * 128) // 128
            src = AP(xn, tt * 128, [[DC * FSB, 128], [FSB, DC], [1, 128]])
            dst = AP(s_xn2t, tile * 128 * D, [[D, 128], [128, DC], [1, 128]])
            P.dma("gpsimd", dst, src, sST, [(sVE, ve_t[DC - 1])])

    ffn_phase(1, N, d_x, p1_epilogue)
    if upto == "p1":
        P.realize()
        return nc, d_dbg

    def p2_phase():
        from contextlib import ExitStack
        with ExitStack() as es:
            wkv = es.enter_context(nc.sbuf_tensor("p2_wkv", [128, DC * 2048], BF16))
            xt = es.enter_context(nc.sbuf_tensor("p2_xt", [128, 2 * D], BF16))
            et = es.enter_context(nc.sbuf_tensor("p2_et", [128, 2 * 1024], BF16))
            vt = es.enter_context(nc.sbuf_tensor("p2_vt", [128, 2 * 1024], BF16))
            ek = es.enter_context(nc.sbuf_tensor("p2_ek", [128, 2048], BF16))
            trg = es.enter_context(nc.sbuf_tensor("p2_trg", [128, 16 * 128], BF16))
            crow = es.enter_context(nc.sbuf_tensor("p2_crow", [1, 1024], F32))
            c128 = es.enter_context(nc.sbuf_tensor("p2_c128", [128, DC], F32))
            ps_kv = es.enter_context(nc.psum_tensor("p2_pskv", [128, 2048], F32))
            ps_T = es.enter_context(nc.psum_tensor("p2_psT", [128, 1024], F32))
            ps_c = es.enter_context(nc.psum_tensor("p2_psc", [1, 1024], F32))

            wld = []
            for dc in range(DC):
                wld.append(P.dma("gpsimd", wkv[:, dc * 2048:dc * 2048 + 1024],
                                 d_wk[dc * 128:(dc + 1) * 128, :], sLG))
                wld.append(P.dma("gpsimd", wkv[:, dc * 2048 + 1024:dc * 2048 + 2048],
                                 d_wv[dc * 128:(dc + 1) * 128, :], sLG))
            ttr = P.dma("sync", AP(trg, 0, [[16 * 128, 128], [128, 16], [1, 128]]),
                        AP(d_trigtp, 0, [[128, 128], [128 * 128, 16], [1, 128]]), sLD)
            P.mm(ps_T[:, 0:512], sb_zero_b[:, 0:128], sb_zero_b[:, 0:512], True, True)
            P.mm(ps_T[:, 512:1024], sb_zero_b[:, 0:128], sb_zero_b[:, 0:512], True, True)

            prev = {}
            for t in range(N // 128):
                par = t % 2
                w = [] if t < 2 else [(sMM, prev[t - 2]["lastmm"])]
                tlx = P.dma("sync", xt[:, par * D:(par + 1) * D],
                            AP(s_xn2t, t * 128 * D, [[D, 128], [1, D]]), sLD, w)
                wv = [(sLD, tlx), (sLG, wld[-1])]
                if t >= 1:
                    wv.append((sSC, prev[t - 1]["esc"]))
                    wv.append((sVE, prev[t - 1]["vve"]))
                tmm = None
                for dc in range(DC):
                    for q in range(4):
                        tmm = P.mm(ps_kv[:, q * 512:(q + 1) * 512],
                                   xt[:, par * D + dc * 128:par * D + (dc + 1) * 128],
                                   wkv[:, dc * 2048 + q * 512:dc * 2048 + (q + 1) * 512],
                                   dc == 0, dc == DC - 1, wv if dc == 0 else ())
                esc = P.emit("scalar", lambda e, par=par: e.activation(
                    out=et[:, par * 1024:(par + 1) * 1024], in_=ps_kv[:, 0:1024],
                    func=AF.Exp), [(sMM, tmm)], sSC)
                vve = P.emit("vector", lambda e, par=par: e.tensor_copy(
                    out=vt[:, par * 1024:(par + 1) * 1024], in_=ps_kv[:, 1024:2048]),
                    [(sMM, tmm)], sVE)
                prev[t] = dict(lastmm=tmm, esc=esc, vve=vve)

                if t < MY // 128:
                    for q in range(2):
                        P.mm(ps_c[0:1, q * 512:(q + 1) * 512], sb_ones_b[:, 0:1],
                             et[:, par * 1024 + q * 512:par * 1024 + (q + 1) * 512],
                             t == 0, t == MY // 128 - 1, [(sSC, esc)])
                    ek_c = AP(ek, 0, [[2048, 128], [128, H], [1, 64]])
                    ek_s = AP(ek, 64, [[2048, 128], [128, H], [1, 64]])
                    e_in = AP(et, par * 1024, [[2 * 1024, 128], [64, H], [1, 64]])
                    cs = AP(trg, t * 128, [[16 * 128, 128], [0, H], [1, 64]])
                    sn = AP(trg, t * 128 + 64, [[16 * 128, 128], [0, H], [1, 64]])
                    P.emit("vector", lambda e, a=ek_c, b=e_in, c=cs: e.tensor_tensor(
                        out=a, in0=b, in1=c, op=ALU.mult),
                        [(sSC, esc), (sLD, ttr)], sVE)
                    tve = P.emit("vector", lambda e, a=ek_s, b=e_in, c=sn: e.tensor_tensor(
                        out=a, in0=b, in1=c, op=ALU.mult), (), sVE)
                    for h in range(H):
                        tmm2 = P.mm(ps_T[:, h * 64:(h + 1) * 64],
                                    ek[:, h * 128:(h + 1) * 128],
                                    vt[:, par * 1024 + h * 64:par * 1024 + (h + 1) * 64],
                                    False, False,
                                    [(sVE, tve), (sVE, vve)] if h == 0 else ())
                    prev[t]["lastmm"] = tmm2
                else:
                    ti = t - MY // 128
                    P.dma("gpsimd", AP(s_etok, ti * 128 * D, [[D, 128], [1, D]]),
                          et[:, par * 1024:(par + 1) * 1024], sST, [(sSC, esc)])
                    P.dma("gpsimd", AP(s_vtok, ti * 128 * D, [[D, 128], [1, D]]),
                          vt[:, par * 1024:(par + 1) * 1024], sST, [(sVE, vve)])
                    if "dbg_e" in d_dbg:
                        P.dma("gpsimd", AP(d_dbg["dbg_e"], ti * 128 * D, [[D, 128], [1, D]]),
                              et[:, par * 1024:(par + 1) * 1024], sST, [(sSC, esc)])
                        P.dma("gpsimd", AP(d_dbg["dbg_v"], ti * 128 * D, [[D, 128], [1, D]]),
                              vt[:, par * 1024:(par + 1) * 1024], sST, [(sVE, vve)])

            tmm_all = P.cnt[sMM]
            tTi = P.emit("vector", lambda e: e.tensor_scalar(
                out=sb_Tinit[:, :], in0=ps_T[:, :], scalar1=gamma_ap, scalar2=None,
                op0=ALU.mult), [(sMM, tmm_all)], sVE)
            tcr = P.emit("vector", lambda e: e.tensor_copy(out=crow[0:1, :], in_=ps_c[0:1, :]),
                         [(sMM, tmm_all)], sVE)
            tst = P.dma("gpsimd", s_crow[0:1, :], crow[0:1, :], sST, [(sVE, tcr)])
            tld = P.dma("sync", c128[:, :], AP(s_crow, 0, [[1, 128], [128, DC]]), sLD,
                        [(sST, tst)], slow=True)
            tcar = P.emit("vector", lambda e: e.tensor_scalar(
                out=sb_carry[:, :], in0=c128[:, :], scalar1=gamma_ap, scalar2=1.0,
                op0=ALU.mult, op1=ALU.add), [(sLD, tld)], sVE)
            if "dbg_T" in d_dbg:
                P.dma("gpsimd", d_dbg["dbg_T"][:, :], sb_Tinit[:, :], sST, [(sVE, tTi)])
                P.dma("gpsimd", d_dbg["dbg_carry"][:, :], sb_carry[:, :], sST, [(sVE, tcar)])

    p2_phase()
    P.barrier()
    if upto == "p2":
        P.realize()
        return nc, d_dbg

    def p4_phase():
        from contextlib import ExitStack
        with ExitStack() as es:
            xn2s = es.enter_context(nc.sbuf_tensor("p4_xn2", [128, DC * ASB], BF16))
            wsl = es.enter_context(nc.sbuf_tensor("p4_wsl", [128, DC * 1024], BF16))
            wg = es.enter_context(nc.sbuf_tensor("p4_wg", [128, 16 * DC + 1024], BF16))
            eT = es.enter_context(nc.sbuf_tensor("p4_eT", [128, DC * ASB], BF16))
            Ssb = es.enter_context(nc.sbuf_tensor("p4_S", [128, DC * ASB], F32))
            qT = es.enter_context(nc.sbuf_tensor("p4_qT", [128, DC * ASB], BF16))
            qnT = es.enter_context(nc.sbuf_tensor("p4_qnT", [128, H * ASB], BF16))
            ekT = es.enter_context(nc.sbuf_tensor("p4_ekT", [128, H * ASB], BF16))
            ektk = es.enter_context(nc.sbuf_tensor("p4_ektk", [128, (ASB // 128) * 2048], BF16))
            vsb = es.enter_context(nc.sbuf_tensor("p4_vsb", [128, (ASB // 128) * 1024], BF16))
            oT = es.enter_context(nc.sbuf_tensor("p4_oT", [128, DC * ASB], BF16))
            gt = qT
            g1s = es.enter_context(nc.sbuf_tensor("p4_g1", [16, ASB], BF16))
            x1s = es.enter_context(nc.sbuf_tensor("p4_x1s", [128, DC * ASB], F32))
            x2s = x1s
            Asb = es.enter_context(nc.sbuf_tensor("p4_A", [128, 2 * 512], BF16))
            trgt = es.enter_context(nc.sbuf_tensor("p4_trgt", [128, 16 * 128], BF16))
            ps_T = es.enter_context(nc.psum_tensor("p4_psT", [128, 1024], F32))
            ps_A = es.enter_context(nc.psum_tensor("p4_psA", [128, 1024], F32))
            ps_o = es.enter_context(nc.psum_tensor("p4_pso", [128, 1024], F32))
            ps_w = es.enter_context(nc.psum_tensor("p4_psw", [128, 1024], F32))

            ttr = P.dma("sync", AP(trgt, 0, [[16 * 128, 128], [128, 16], [1, 128]]),
                        AP(d_trigtm, 0, [[128, 128], [128 * 128, 16], [1, 128]]), sLD)
            t0 = P.mm(ps_T[:, 0:512], sb_id[:, :], sb_Tinit[:, 0:512], True, True)
            t0 = P.mm(ps_T[:, 512:1024], sb_id[:, :], sb_Tinit[:, 512:1024], True, True)
            snap = {0: P.emit("vector", lambda e: e.tensor_copy(
                out=sb_Tsnap[:, 0:1024], in_=ps_T[:, :]), [(sMM, t0)], sVE)}
            evo_t = None

            for sb in range(MY // ASB):
                col0 = MY + sb * ASB
                ldxn = [P.dma("sync", xn2s[:, ch * ASB:(ch + 1) * ASB],
                              s_xn2[ch * 128:(ch + 1) * 128, col0:col0 + ASB], sLD)
                        for ch in range(DC)]
                wld = [P.dma("gpsimd", wsl[:, dc * 1024:(dc + 1) * 1024],
                             d_wk[dc * 128:(dc + 1) * 128, :], sLG) for dc in range(DC)]
                tvld, tele = [], []
                for c in range(ASB // 128):
                    ti = sb * (ASB // 128) + c
                    dst = AP(ektk, c * 2048, [[(ASB // 128) * 2048, 128], [128, H], [1, 64]])
                    src = AP(s_etok, ti * 128 * D, [[D, 128], [64, H], [1, 64]])
                    tele.append(P.dma("sync", dst, src, sLD))
                    tvld.append(P.dma("sync", vsb[:, c * 1024:(c + 1) * 1024],
                                      AP(s_vtok, ti * 128 * D, [[D, 128], [1, D]]), sLD))
                esc = []
                for ob in range(DC):
                    par = ob % 2
                    wv = [(sLD, ldxn[0]), (sLG, wld[DC - 1])]
                    if ob >= 2:
                        wv.append((sSC, esc[ob - 2]))
                    for dc in range(DC):
                        P.mm(ps_w[:, par * 512:(par + 1) * 512],
                             wsl[:, dc * 1024 + ob * 128:dc * 1024 + (ob + 1) * 128],
                             xn2s[:, dc * ASB:(dc + 1) * ASB],
                             dc == 0, dc == DC - 1,
                             wv if dc == 0 else [(sLD, ldxn[dc])])
                    tmm = P.cnt[sMM]
                    esc.append(P.emit("scalar", lambda e, ob=ob, par=par: e.activation(
                        out=eT[:, ob * ASB:(ob + 1) * ASB],
                        in_=ps_w[:, par * 512:(par + 1) * 512], func=AF.Exp),
                        [(sMM, tmm)], sSC))
                tscan = []
                for ch in range(DC):
                    P.emit("vector", lambda e, ch=ch: e.tensor_tensor_scan(
                        out=Ssb[:, ch * ASB:(ch + 1) * ASB],
                        data0=eT[:, ch * ASB:(ch + 1) * ASB],
                        data1=eT[:, ch * ASB:(ch + 1) * ASB],
                        initial=sb_carry[:, ch:ch + 1],
                        op0=ALU.add, op1=ALU.bypass), [(sSC, esc[ch])], sVE)
                    P.emit("vector", lambda e, ch=ch: e.tensor_copy(
                        out=sb_carry[:, ch:ch + 1],
                        in_=Ssb[:, ch * ASB + ASB - 1:ch * ASB + ASB]), (), sVE)
                    tscan.append(P.emit("vector", lambda e, ch=ch: e.reciprocal(
                        out=Ssb[:, ch * ASB:(ch + 1) * ASB],
                        in_=Ssb[:, ch * ASB:(ch + 1) * ASB]), (), sVE))
                tekT = []
                for h in range(H):
                    ch, hi = h // 2, h % 2
                    w = [(sSC, esc[ch])]
                    if h >= 2:
                        w.append((sVE, tekT[h - 2]))
                    tdm = P.mm(ps_A[:, (h % 2) * 512:(h % 2) * 512 + 512],
                               sb_dup[:, hi * 128:(hi + 1) * 128],
                               eT[:, ch * ASB:(ch + 1) * ASB], True, True, w)
                    if sb == 0 and h == 0 and "dbg_dup" in d_dbg:
                        tcp = P.emit("vector", lambda e: e.tensor_copy(
                            out=Ssb[:, 0:512].bitcast(F32) if False else x1s[:, 0:512],
                            in_=ps_A[:, 0:512]), [(sMM, tdm)], sVE)
                        P.dma("gpsimd", d_dbg["dbg_dup"][:, :], x1s[:, 0:512], sST,
                              [(sVE, tcp)])
                        P.dma("gpsimd", d_dbg["dbg_trigf"][:, :], sb_trigf[:, :], sST)
                        P.dma("gpsimd", d_dbg["dbg_eT"][:, :], eT[:, :], sST,
                              [(sSC, esc[DC - 1])])
                    tekT.append(P.emit("vector", lambda e, h=h, sb=sb: e.tensor_tensor(
                        out=ekT[:, h * ASB:(h + 1) * ASB],
                        in0=ps_A[:, (h % 2) * 512:(h % 2) * 512 + 512],
                        in1=sb_trigf[:, sb * ASB:(sb + 1) * ASB], op=ALU.mult),
                        [(sMM, tdm)], sVE))
                wld2 = [P.dma("gpsimd", wsl[:, dc * 1024:(dc + 1) * 1024],
                              d_wq[dc * 128:(dc + 1) * 128, :], sLG,
                              [(sMM, P.cnt[sMM])]) for dc in range(DC)]
                qsc = []
                for ob in range(DC):
                    par = ob % 2
                    wv = [(sLG, wld2[DC - 1])]
                    wv.append((sSC, esc[DC - 2 + ob] if ob < 2 else qsc[ob - 2]))
                    for dc in range(DC):
                        P.mm(ps_w[:, par * 512:(par + 1) * 512],
                             wsl[:, dc * 1024 + ob * 128:dc * 1024 + (ob + 1) * 128],
                             xn2s[:, dc * ASB:(dc + 1) * ASB],
                             dc == 0, dc == DC - 1, wv if dc == 0 else ())
                    tmm = P.cnt[sMM]
                    qsc.append(P.emit("scalar", lambda e, ob=ob, par=par: e.activation(
                        out=qT[:, ob * ASB:(ob + 1) * ASB],
                        in_=ps_w[:, par * 512:(par + 1) * 512], func=AF.Silu),
                        [(sMM, tmm)], sSC))
                tqr = []
                for ch in range(DC):
                    tqr.append(P.emit("vector", lambda e, ch=ch: e.tensor_tensor(
                        out=qT[:, ch * ASB:(ch + 1) * ASB], in0=qT[:, ch * ASB:(ch + 1) * ASB],
                        in1=Ssb[:, ch * ASB:(ch + 1) * ASB], op=ALU.mult),
                        [(sSC, qsc[ch]), (sVE, tscan[ch])], sVE))
                tqnT = []
                for h in range(H):
                    ch, hi = h // 2, h % 2
                    w = [(sVE, tqr[ch])]
                    if h >= 2:
                        w.append((sVE, tqnT[h - 2]))
                    tdm = P.mm(ps_A[:, (h % 2) * 512:(h % 2) * 512 + 512],
                               sb_dup[:, hi * 128:(hi + 1) * 128],
                               qT[:, ch * ASB:(ch + 1) * ASB], True, True, w)
                    tqnT.append(P.emit("vector", lambda e, h=h, sb=sb: e.tensor_tensor(
                        out=qnT[:, h * ASB:(h + 1) * ASB],
                        in0=ps_A[:, (h % 2) * 512:(h % 2) * 512 + 512],
                        in1=sb_trigf[:, sb * ASB:(sb + 1) * ASB], op=ALU.mult),
                        [(sMM, tdm)], sVE))
                tekt = []
                for c in range(ASB // 128):
                    ek_c = AP(ektk, c * 2048, [[(ASB // 128) * 2048, 128], [128, H], [1, 64]])
                    ek_s = AP(ektk, c * 2048 + 64, [[(ASB // 128) * 2048, 128], [128, H], [1, 64]])
                    cs = AP(trgt, (sb * (ASB // 128) + c) * 128,
                            [[16 * 128, 128], [0, H], [1, 64]])
                    sn = AP(trgt, (sb * (ASB // 128) + c) * 128 + 64,
                            [[16 * 128, 128], [0, H], [1, 64]])
                    P.emit("vector", lambda e, a=ek_s, b=ek_c, c_=sn: e.tensor_tensor(
                        out=a, in0=b, in1=c_, op=ALU.mult),
                        [(sLD, tele[c]), (sLD, ttr)], sVE)
                    tekt.append(P.emit("vector", lambda e, a=ek_c, c_=cs: e.tensor_tensor(
                        out=a, in0=a, in1=c_, op=ALU.mult), (), sVE))

                bank_reader = {0: tqnT[H - 1], 1: tqnT[H - 1]}
                asb_reader = {0: 0, 1: 0}
                for c in range(ASB // 128):
                    k = sb * (ASB // 128) + c
                    wz = [(sVE, snap[k])]
                    if evo_t is not None:
                        wz.append((sVE, evo_t))
                    tz = P.mm(ps_o[:, 0:512], sb_zero_b[:, 0:128], sb_zero_b[:, 0:512],
                              True, True, wz)
                    tz = P.mm(ps_o[:, 512:1024], sb_zero_b[:, 0:128], sb_zero_b[:, 0:512],
                              True, True)
                    first = True
                    for grp in range(4):
                        bk = grp % 2
                        tA = None
                        for gp in range(4):
                            h = grp * 4 + gp
                            w = [(sVE, tqnT[h]), (sVE, tekT[h])]
                            if gp == 0:
                                w.append((sVE, bank_reader[bk]))
                            tA = P.mm(ps_A[:, bk * 512 + gp * 128:bk * 512 + (gp + 1) * 128],
                                      ekT[:, h * ASB + c * 128:h * ASB + (c + 1) * 128],
                                      qnT[:, h * ASB + c * 128:h * ASB + (c + 1) * 128],
                                      True, True, w)
                        tev = P.emit("vector", lambda e, bk=bk: e.tensor_tensor(
                            out=Asb[:, bk * 512:bk * 512 + 512],
                            in0=ps_A[:, bk * 512:bk * 512 + 512],
                            in1=sb_mask[:, :], op=ALU.mult),
                            [(sMM, tA), (sMM, asb_reader[bk])], sVE)
                        bank_reader[bk] = tev
                        for gp in range(4):
                            h = grp * 4 + gp
                            P.mm(ps_o[64 * (h % 2):64 * (h % 2) + 64,
                                      (h // 2) * 128:(h // 2) * 128 + 128],
                                 sb_Tsnap[:, (k % 2) * 1024 + h * 64:(k % 2) * 1024 + (h + 1) * 64],
                                 qnT[:, h * ASB + c * 128:h * ASB + (c + 1) * 128],
                                 False, False, [(sMM, tz)] if first else (),
                                 tile_position=(0, 64 * (h % 2)))
                            first = False
                            P.mm(ps_o[64 * (h % 2):64 * (h % 2) + 64,
                                      (h // 2) * 128:(h // 2) * 128 + 128],
                                 vsb[:, c * 1024 + h * 64:c * 1024 + (h + 1) * 64],
                                 Asb[:, bk * 512 + gp * 128:bk * 512 + (gp + 1) * 128],
                                 False, False,
                                 [(sVE, tev), (sLD, tvld[c])],
                                 tile_position=(0, 64 * (h % 2)))
                        asb_reader[bk] = P.cnt[sMM]
                    tst = None
                    for h in range(H):
                        tst = P.mm(ps_T[:, h * 64:(h + 1) * 64],
                                   AP(ektk, c * 2048 + h * 128,
                                      [[(ASB // 128) * 2048, 128], [1, 128]]),
                                   vsb[:, c * 1024 + h * 64:c * 1024 + (h + 1) * 64],
                                   False, False,
                                   [(sVE, tekt[c]), (sVE, snap[k])] if h == 0 else ())
                    tin = P.cnt[sMM]
                    evo_t = P.emit("vector", lambda e, c=c: e.tensor_copy(
                        out=AP(oT, c * 128, [[DC * ASB, 128], [ASB, DC], [1, 128]]),
                        in_=AP(ps_o, 0, [[1024, 128], [128, DC], [1, 128]])),
                        [(sMM, tin)], sVE)
                    snap[k + 1] = P.emit("vector", lambda e, k=k: e.tensor_copy(
                        out=sb_Tsnap[:, ((k + 1) % 2) * 1024:((k + 1) % 2) * 1024 + 1024],
                        in_=ps_T[:, :]), [(sMM, tst)], sVE)

                dump_t = None
                if sb == 0 and "dbg_oT" in d_dbg:
                    P.dma("gpsimd", d_dbg["dbg_qnT"][:, :], qnT[:, :], sST, [(sVE, tqnT[H - 1])])
                    P.dma("gpsimd", d_dbg["dbg_ekT"][:, :], ekT[:, :], sST, [(sVE, tekT[H - 1])])
                    P.dma("gpsimd", d_dbg["dbg_R"][:, :], Ssb[:, :], sST, [(sVE, tscan[DC - 1])])
                    P.dma("gpsimd", d_dbg["dbg_ektk"][:, :], ektk[:, :], sST, [(sVE, tekt[-1])])
                    dump_t = P.dma("gpsimd", d_dbg["dbg_oT"][:, :], oT[:, :], sST, [(sVE, evo_t)])
                wg1t = P.dma("gpsimd", AP(wg, 0, [[16 * DC + 1024, 128], [16, DC], [1, 16]]),
                             AP(d_wg1, 0, [[H, 128], [128 * H, DC], [1, H]]), sLG)
                wg2t = P.dma("gpsimd", wg[0:16, 16 * DC:16 * DC + 1024], d_wg2[:, :], sLG)
                tg1 = None
                for dc in range(DC):
                    tg1 = P.mm(ps_w[0:16, 0:ASB], wg[:, dc * 16:(dc + 1) * 16],
                               xn2s[:, dc * ASB:(dc + 1) * ASB], dc == 0, dc == DC - 1,
                               [(sLG, wg1t), (sSC, qsc[DC - 2]), (sSC, qsc[DC - 1])]
                               if dc == 0 else ())
                tg1v = P.emit("vector", lambda e: e.tensor_copy(out=g1s[:, :],
                                                                in_=ps_w[0:16, 0:ASB]),
                              [(sMM, tg1)], sVE)
                gsc = []
                for ob in range(DC):
                    par = ob % 2
                    w = [(sLG, wg2t), (sVE, tg1v)]
                    if ob >= 2:
                        w.append((sSC, gsc[ob - 2]))
                    tg2 = P.mm(ps_w[:, par * 512:(par + 1) * 512],
                               wg[0:16, 16 * DC + ob * 128:16 * DC + (ob + 1) * 128],
                               g1s[:, :], True, True, w)
                    gsc.append(P.emit("scalar", lambda e, ob=ob, par=par: e.activation(
                        out=gt[:, ob * ASB:(ob + 1) * ASB],
                        in_=ps_w[:, par * 512:(par + 1) * 512], func=AF.Sigmoid),
                        [(sMM, tg2)], sSC))
                if sb == 0 and "dbg_gate" in d_dbg:
                    P.dma("gpsimd", d_dbg["dbg_gate"][:, :], gt[:, :], sST, [(sSC, gsc[DC - 1])])
                tog, tsq = [], []
                for ch in range(DC):
                    w_og = [(sSC, gsc[ch]), (sVE, evo_t)]
                    if dump_t is not None:
                        w_og.append((sST, dump_t))
                    tog.append(P.emit("vector", lambda e, ch=ch: e.tensor_tensor(
                        out=oT[:, ch * ASB:(ch + 1) * ASB], in0=oT[:, ch * ASB:(ch + 1) * ASB],
                        in1=gt[:, ch * ASB:(ch + 1) * ASB], op=ALU.mult),
                        w_og, sVE))
                for ch in range(DC):
                    tsq.append(P.emit("vector", lambda e, ch=ch: e.tensor_tensor(
                        out=gt[:, ch * ASB:(ch + 1) * ASB], in0=oT[:, ch * ASB:(ch + 1) * ASB],
                        in1=oT[:, ch * ASB:(ch + 1) * ASB], op=ALU.mult), [(sVE, tog[ch])], sVE))
                tms = None
                for ch in range(DC):
                    tms = P.mm(ps_A[0:1, 0:512], sb_ones_b[:, 0:1],
                               gt[:, ch * ASB:(ch + 1) * ASB], ch == 0, ch == DC - 1,
                               [(sVE, tsq[ch])])
                Rr = sb_rows2[0:1, 1024:1024 + ASB]
                P.emit("vector", lambda e: e.tensor_scalar(
                    out=Rr, in0=ps_A[0:1, 0:512], scalar1=1.0 / D, scalar2=RMS_EPS,
                    op0=ALU.mult, op1=ALU.add), [(sMM, tms)], sVE)
                trs = P.emit("scalar", lambda e: e.activation(out=Rr, in_=Rr, func=AF.Sqrt),
                             [(sVE, P.cnt[sVE])], sSC)
                trv = P.emit("vector", lambda e: e.reciprocal(out=Rr, in_=Rr),
                             [(sSC, trs)], sVE)
                tfm = P.mm(ps_A[:, 512:1024], sb_onesrow[0:1, 0:128], Rr, True, True,
                           [(sVE, trv)])
                tno = []
                for ch in range(DC):
                    tv = P.emit("vector", lambda e, ch=ch: e.tensor_tensor(
                        out=oT[:, ch * ASB:(ch + 1) * ASB], in0=oT[:, ch * ASB:(ch + 1) * ASB],
                        in1=ps_A[:, 512:1024], op=ALU.mult), [(sMM, tfm)], sVE)
                    tno.append(P.emit("scalar", lambda e, ch=ch: e.activation(
                        out=oT[:, ch * ASB:(ch + 1) * ASB], in_=oT[:, ch * ASB:(ch + 1) * ASB],
                        func=AF.Identity, bias=0.0, scale=sb_par[:, 120 + ch:121 + ch]),
                        [(sVE, tv)], sSC))
                wld3 = [P.dma("gpsimd", wsl[:, dc * 1024:(dc + 1) * 1024],
                              d_wo[dc * 128:(dc + 1) * 128, :], sLG, [(sMM, tfm)])
                        for dc in range(DC)]
                ldx1 = [P.dma("sync", x1s[:, ch * ASB:(ch + 1) * ASB],
                              s_x1[ch * 128:(ch + 1) * 128, col0:col0 + ASB], sLD)
                        for ch in range(DC)]
                tx2 = [None] * DC
                for ob in range(DC):
                    par = ob % 2
                    wv = [(sLG, wld3[DC - 1]), (sSC, tno[DC - 1])]
                    if ob >= 2:
                        wv.append((sVE, tx2[ob - 2]))
                    if ob < 2:
                        wv.append((sSC, gsc[DC - 2 + ob]))
                    for ic in range(DC):
                        P.mm(ps_w[:, par * 512:(par + 1) * 512],
                             wsl[:, ic * 1024 + ob * 128:ic * 1024 + (ob + 1) * 128],
                             oT[:, ic * ASB:(ic + 1) * ASB], ic == 0, ic == DC - 1,
                             wv if ic == 0 else ())
                    tmm = P.cnt[sMM]
                    tx2[ob] = P.emit("vector", lambda e, ob=ob, par=par: e.tensor_tensor(
                        out=x2s[:, ob * ASB:(ob + 1) * ASB],
                        in0=ps_w[:, par * 512:(par + 1) * 512],
                        in1=x1s[:, ob * ASB:(ob + 1) * ASB], op=ALU.add),
                        [(sMM, tmm), (sLD, ldx1[ob])], sVE)
                for ch in range(DC):
                    P.dma("gpsimd", s_x2[ch * 128:(ch + 1) * 128, sb * ASB:(sb + 1) * ASB],
                          x2s[:, ch * ASB:(ch + 1) * ASB], sST, [(sVE, tx2[ch])])
                    if "dbg_x2" in d_dbg:
                        P.dma("gpsimd",
                              d_dbg["dbg_x2"][ch * 128:(ch + 1) * 128, sb * ASB:(sb + 1) * ASB],
                              x2s[:, ch * ASB:(ch + 1) * ASB], sST, [(sVE, tx2[ch])])
                P.barrier()

    p4_phase()
    if upto == "p4":
        P.realize()
        return nc, d_dbg

    def p5_epilogue(sb, col0, bufs, x1_t):
        x1blk = bufs["x1blk"]
        for ch in range(DC):
            P.dma("gpsimd", d_out[ch * 128:(ch + 1) * 128, col0:col0 + FSB],
                  x1blk[:, ch * FSB:(ch + 1) * FSB], sST, [(sVE, x1_t[ch])])

    ffn_phase(2, MY, s_x2, p5_epilogue)
    P.barrier()
    P.realize()
    return nc, d_dbg



def host_inputs(inputs, core):
    b, half = core // 2, core % 2
    x = np.asarray(inputs["L0_attn_x"])[b]
    xcat = np.concatenate([x[0:MY], x[half * MY:(half + 1) * MY]], axis=0)
    xT = np.ascontiguousarray(xcat.T.astype(np.float32))

    j = np.arange(HD, dtype=np.float64)
    inv = BASE ** (-2.0 * j / HD)
    th_own = (np.arange(half * MY, half * MY + MY, dtype=np.float64)[:, None] * inv[None, :])
    th_pre = (np.arange(0, MY, dtype=np.float64)[:, None] * inv[None, :])
    cos_o, sin_o = np.cos(th_own), np.sin(th_own)
    cos_p, sin_p = np.cos(th_pre), np.sin(th_pre)

    def fc(v):
        v = np.asarray(v, dtype=np.float32)
        return np.ascontiguousarray(v.reshape(-1, 128).T)

    par = np.zeros((128, 129), dtype=np.float32)
    par[:, 0:8] = fc(inputs["L0_ffn_ln1_g"])
    par[:, 8:16] = fc(inputs["L0_ffn_ln1_b"])
    par[:, 16:48] = fc(inputs["L0_ffn_b1"])
    par[:, 48:80] = fc(inputs["L0_ffn_ln2_g"])
    par[:, 80:112] = fc(inputs["L0_ffn_ln2_b"])
    par[:, 112:120] = fc(inputs["L0_ffn_b2"])
    par[:, 120:128] = fc(inputs["L0_attn_norm_w"])
    par[:, 128] = float(half)

    dup0 = np.zeros((128, 128), dtype=np.float32)
    dup1 = np.zeros((128, 128), dtype=np.float32)
    for m in range(128):
        dup0[m % 64, m] = 1.0
        dup1[64 + (m % 64), m] = 1.0

    bf = ml_dtypes.bfloat16
    return {
        "xT": xT,
        "Wq": np.asarray(inputs["L0_attn_Wq"], np.float32),
        "Wk": np.asarray(inputs["L0_attn_Wk"], np.float32),
        "Wv": np.asarray(inputs["L0_attn_Wv"], np.float32),
        "Wo": np.asarray(inputs["L0_attn_Wo"], np.float32),
        "Wg1": np.asarray(inputs["L0_attn_Wg1"], np.float32),
        "Wg2": np.asarray(inputs["L0_attn_Wg2"], np.float32),
        "w1": np.ascontiguousarray(np.asarray(inputs["L0_ffn_w1"], np.float32).T),
        "w2": np.ascontiguousarray(np.asarray(inputs["L0_ffn_w2"], np.float32).T),
        "par": par,
        "b2row": np.asarray(inputs["L0_ffn_b2"], np.float32).reshape(1, D),
        "trigf": np.concatenate([cos_o.T, sin_o.T], 0).astype(np.float32),
        "trigtm": np.concatenate([cos_o, sin_o], 1).astype(bf),
        "trigtp": np.concatenate([cos_p, sin_p], 1).astype(bf),
        "mask": np.triu(np.ones((128, 128), np.float32)).astype(bf),
        "ident": np.eye(128, dtype=np.float32),
        "dup0": dup0.astype(bf),
        "dup1": dup1.astype(bf),
    }


def run(inputs, upto="full", debug=(), cores=None):
    nc, d_dbg = build_program(upto=upto, debug=debug)
    cores = cores if cores is not None else list(range(NCORES))
    in_maps = [host_inputs(inputs, c) for c in cores]
    res = run_bass_kernel_spmd(nc, in_maps, core_ids=cores)
    return res


def kernel(**inputs):
    res = run(inputs)
    out = np.zeros((B, N, D), dtype=np.float32)
    for c in range(NCORES):
        b, half = c // 2, c % 2
        out[b, half * MY:(half + 1) * MY] = res.results[c]["out"].T
    return out
